# version 25
# baseline (speedup 1.0000x reference)
import numpy as np
import ml_dtypes
from contextlib import ExitStack
import concourse.bass as bass
import concourse.mybir as mybir
from concourse.bass_utils import run_bass_kernel_spmd

F32 = mybir.dt.float32
BF16 = mybir.dt.bfloat16
AF = mybir.ActivationFunctionType
ALU = mybir.AluOpType
AXX = mybir.AxisListType.X

B, T, D = 2, 8192, 1024
OWN = 2048
TT = OWN // 128
GROUPS = [[0, 1, 2, 3], [4, 5, 6, 7]]
LN_EPS = 1e-5
A_EPS = 1e-6

_BUILD_PHASES = 99
_A_STAGE = 99
_F_STAGE = 99
_B_STAGE = 99
_DUMP = None
_DUMP_BUF = None



def _stage(inputs):
    x_full = np.asarray(inputs["L0_attn_a_x"], np.float32)
    wqkv = np.asarray(inputs["L0_attn_b_w_qkv"], np.float32)
    bqkv = np.asarray(inputs["L0_attn_b_b_qkv"], np.float32)
    per_core = []
    for c in range(8):
        b, p = c // 4, c % 4
        heads0 = 4 * p
        cols = np.concatenate([np.arange(part * D + heads0 * 64,
                                         part * D + heads0 * 64 + 256)
                               for part in range(3)])
        expn = np.zeros((1, 4), np.float32)
        flg = np.zeros((1, 4), np.float32)
        for r in range(4):
            if r < p:
                expn[0, r] = 2048.0 * (p - 1 - r)
                flg[0, r] = 1.0
        gix = np.zeros((128, 8), np.int32)
        for ft in range(8):
            gix[:, ft] = 1024 * p + 128 * ft + np.arange(128)
        d = {
            "x": np.ascontiguousarray(x_full[b, 2048 * p:2048 * (p + 1)]),
            "bqvr": np.ascontiguousarray(bqkv[cols[512:768]].reshape(1, 256)),
            "gix": gix,
            "wq": np.asarray(inputs["L0_attn_a_W_Q"], np.float32),
            "wk": np.asarray(inputs["L0_attn_a_W_K"], np.float32),
            "wv": np.asarray(inputs["L0_attn_a_W_V"], np.float32),
            "wg": np.asarray(inputs["L0_attn_a_W_g_w"], np.float32),
            "wo": np.asarray(inputs["L0_attn_a_W_O"], np.float32),
            "dl": np.asarray(inputs["L0_attn_a_decay_log"], np.float32).reshape(1, 8),
            "wqkv": np.ascontiguousarray(wqkv[:, cols]),
            "bqkv": np.ascontiguousarray(bqkv[cols].reshape(6, 128).T),
            "wproj": np.asarray(inputs["L0_attn_b_w_proj"], np.float32),
            "theta": np.asarray(inputs["L0_ffn_theta"], np.float32),
            "phi": np.asarray(inputs["L0_ffn_phi"], np.float32),
            "thn": np.asarray(inputs["L0_ffn_theta_noise"], np.float32),
            "phn": np.asarray(inputs["L0_ffn_phi_noise"], np.float32),
            "wgbr": np.asarray(inputs["L0_attn_a_W_g_b"], np.float32).reshape(1, 512),
            "bpr": np.asarray(inputs["L0_attn_b_b_proj"], np.float32).reshape(1, 1024),
            "expn": expn,
            "flg": flg,
        }
        per_core.append(d)
    return per_core


def kernel(**inputs):
    shards = _stage(inputs)
    nc = _build()
    res = run_bass_kernel_spmd(nc, shards, core_ids=list(range(8)))
    out = np.zeros((B, T, D), np.float32)
    for c in range(8):
        b, p = c // 4, c % 4
        out[b, 2048 * p:2048 * (p + 1)] = np.asarray(res.results[c]["out"], np.float32)
    kernel._last = res
    return out



class _NullBlock:
    def __enter__(self):
        return self

    def __exit__(self, *a):
        return False

    def _noop(self, f):
        return None
    sync = scalar = vector = tensor = gpsimd = _noop


def _build():
    nc = bass.Bass("TRN2", target_bir_lowering=False,
                   dynamic_dma_scratch_size=16384)
    PH = _BUILD_PHASES

    x_d = nc.dram_tensor("x", [OWN, D], F32, kind="ExternalInput")
    wq_d = nc.dram_tensor("wq", [512, 1024], F32, kind="ExternalInput")
    wk_d = nc.dram_tensor("wk", [512, 1024], F32, kind="ExternalInput")
    wv_d = nc.dram_tensor("wv", [512, 1024], F32, kind="ExternalInput")
    wg_d = nc.dram_tensor("wg", [512, 1024], F32, kind="ExternalInput")
    wo_d = nc.dram_tensor("wo", [1024, 512], F32, kind="ExternalInput")
    dl_d = nc.dram_tensor("dl", [1, 8], F32, kind="ExternalInput")
    wqkv_d = nc.dram_tensor("wqkv", [1024, 768], F32, kind="ExternalInput")
    bqkv_d = nc.dram_tensor("bqkv", [128, 6], F32, kind="ExternalInput")
    wproj_d = nc.dram_tensor("wproj", [1024, 1024], F32, kind="ExternalInput")
    th_d = nc.dram_tensor("theta", [1024, 1024], F32, kind="ExternalInput")
    ph_d = nc.dram_tensor("phi", [1024, 1024], F32, kind="ExternalInput")
    thn_d = nc.dram_tensor("thn", [1024, 1024], F32, kind="ExternalInput")
    phn_d = nc.dram_tensor("phn", [1024, 1024], F32, kind="ExternalInput")
    wgbr_d = nc.dram_tensor("wgbr", [1, 512], F32, kind="ExternalInput")
    bpr_d = nc.dram_tensor("bpr", [1, 1024], F32, kind="ExternalInput")
    expn_d = nc.dram_tensor("expn", [1, 4], F32, kind="ExternalInput")
    flg_d = nc.dram_tensor("flg", [1, 4], F32, kind="ExternalInput")
    out_d = nc.dram_tensor("out", [OWN, D], F32, kind="ExternalOutput")

    bqvr_d = nc.dram_tensor("bqvr", [1, 256], F32, kind="ExternalInput")
    gix_d = nc.dram_tensor("gix", [128, 8], mybir.dt.int32, kind="ExternalInput")

    aga_in = nc.dram_tensor("aga_in", [64, 520], F32)
    aga_out = nc.dram_tensor("aga_out", [256, 520], F32)
    ag1_in = nc.dram_tensor("ag1_in", [4096, 512], BF16)
    ag1_out = nc.dram_tensor("ag1_out", [16384, 512], BF16)
    q_scr = nc.dram_tensor("q_scr", [256, 8192], BF16)
    ag2_in = nc.dram_tensor("ag2_in", [1024, OWN], BF16)
    ag2_out = nc.dram_tensor("ag2_out", [4096, OWN], BF16)

    dbg = {}
    ctx = ExitStack()
    E = ctx.enter_context

    def sb(name, shape, dtype):
        return E(nc.sbuf_tensor(name, shape, dtype))

    sems = {}

    def sem(name):
        if name not in sems:
            sems[name] = E(nc.semaphore(name))
        return sems[name]

    x_sb = sb("x_sb", [128, TT * 1024], F32)
    xv = x_sb[:, :].rearrange("p (t f) -> p t f", f=1024)
    xnT = sb("xnT", [128, 8 * OWN], BF16)
    xnTv = xnT[:, :].rearrange("p (ft t) -> p ft t", t=OWN)
    xns2 = sb("xns2", [128, 2 * 1024], BF16)

    ident = sb("ident", [128, 128], BF16)
    tri01 = sb("tri01", [128, 128], BF16)
    trif = sb("trif", [128, 128], BF16)
    emat = sb("emat", [128, 128], BF16)
    iotac = sb("iotac", [128, 1], F32)
    ones1f = sb("ones1f", [1, 128], F32)
    onesc = sb("onesc", [128, 1], F32)
    onesden = sb("onesden", [128, 64], BF16)
    scr1 = sb("scr1", [128, 128], BF16)
    c_lneps = sb("c_lneps", [128, 1], F32)
    c_pihalf = sb("c_pihalf", [128, 1], F32)

    ln_sums = sb("ln_sums", [128, TT], F32)
    ln_mneg = sb("ln_mneg", [128, TT], F32)
    ln_var = sb("ln_var", [128, TT], F32)
    ln_rstd = sb("ln_rstd", [128, TT], F32)
    sqdump = sb("sqdump", [128, 1024], BF16)

    ldw = sem("ldw")
    _ldw_n = [0]

    def dma(eng, out, in_):
        eng.dma_start(out=out, in_=in_).then_inc(ldw, 16)
        _ldw_n[0] += 16
        return _ldw_n[0]

    x_ready = [0]
    with nc.Block("ph0") as blk:
        @blk.sync
        def _(sync):
            x_ready[0] = dma(sync, xv, x_d.rearrange("(t p) f -> p t f", p=128))

        @blk.gpsimd
        def _(g):
            s0 = sem("c0chain")
            k = [0]

            def step(ins):
                ins.then_inc(s0, 1)
                k[0] += 1
                g.wait_ge(s0, k[0])
            step(g.memset(scr1[:, :], 1.0))
            step(g.affine_select(out=scr1[:, :], in_=scr1[:, :],
                                 compare_op=ALU.is_ge, fill=0.0, base=0,
                                 pattern=[[1, 128]], channel_multiplier=-1))
            step(g.tensor_copy(tri01[:, :], scr1[:, :]))
            step(g.tensor_copy(trif[:, :], scr1[:, :]))
            step(g.affine_select(out=scr1[:, :], in_=scr1[:, :],
                                 compare_op=ALU.is_ge, fill=0.0, base=0,
                                 pattern=[[-1, 128]], channel_multiplier=1))
            step(g.tensor_copy(ident[:, :], scr1[:, :]))
            g.iota(emat[:, :], [[1, 128]], base=0, channel_multiplier=-1,
                   allow_small_or_imprecise_dtypes=True)
            g.iota(iotac[:, :], [[1, 1]], base=0, channel_multiplier=1,
                   allow_small_or_imprecise_dtypes=True)
            g.memset(ones1f[:, :], 1.0)
            g.memset(onesc[:, :], 1.0)
            g.memset(onesden[:, :], 1.0)
            g.memset(c_lneps[:, :], LN_EPS)
            g.memset(c_pihalf[:, :], float(np.pi / 2)).then_inc(sem("tab0"), 1)

    ln_count = [0]

    def emit_ln(tag, waits, tile_waits=None, keep_reg=False):
        i = ln_count[0]
        ln_count[0] += 1
        s_v1 = sem("ln_v1")
        s_a1 = sem("ln_a1")
        s_nm, s_tp, s_cp = sem("ln_nm"), sem("ln_tp"), sem("ln_cp")
        bV, bA, bT = i * 2 * TT, i * 3 * TT, i * TT
        xns2v = xns2[:, :].rearrange("p (b f) -> p b f", f=1024)
        reg_ln = ExitStack()
        ps_ln = [reg_ln.enter_context(
            nc.psum_tensor(f"ps_ln{i}_{k}", [128, 1024], BF16)) for k in range(2)]

        with nc.Block(f"ln_{tag}") as blk:
            @blk.vector
            def _(v):
                for s, n in waits:
                    v.wait_ge(s, n)

                def stats(t):
                    if tile_waits is not None:
                        for s, n in tile_waits(t):
                            v.wait_ge(s, n)
                    v.reduce_sum(ln_sums[:, t:t + 1], xv[:, t, :], axis=AXX
                                 ).then_inc(s_v1, 1)
                    v.wait_ge(s_v1, bV + 2 * t + 1)
                    v.tensor_scalar_mul(ln_mneg[:, t:t + 1],
                                        ln_sums[:, t:t + 1], -1.0 / 1024
                                        ).then_inc(s_v1, 1)

                def norm(t):
                    v.wait_ge(s_a1, bA + 3 * t + 3)
                    if t >= 2:
                        v.wait_ge(s_tp, bT + t - 1)
                    v.tensor_scalar(xns2v[:, t % 2, :], xv[:, t, :],
                                    ln_mneg[:, t:t + 1], ln_rstd[:, t:t + 1],
                                    ALU.add, ALU.mult).then_inc(s_nm, 1)

                def copy(t):
                    v.wait_ge(s_tp, bT + t + 1)
                    v.tensor_copy(xnTv[:, :, t * 128:(t + 1) * 128],
                                  ps_ln[t % 2][:, :].rearrange(
                                      "p (ft c) -> p ft c", c=128)
                                  ).then_inc(s_cp, 1)
                for t in range(TT):
                    stats(t)
                    if t >= 1:
                        norm(t - 1)
                    if t >= 2:
                        copy(t - 2)
                norm(TT - 1)
                copy(TT - 2)
                copy(TT - 1)

            @blk.scalar
            def _(a):
                for s, n in waits:
                    a.wait_ge(s, n)
                a.wait_ge(sem("tab0"), 1)
                for t in range(TT):
                    if tile_waits is not None:
                        for s, n in tile_waits(t):
                            a.wait_ge(s, n)
                    a.wait_ge(s_v1, bV + 2 * t + 2)
                    a.activation(sqdump[:, :], xv[:, t, :], AF.Square,
                                 bias=ln_mneg[:, t:t + 1], scale=1.0,
                                 accum_out=ln_var[:, t:t + 1]).then_inc(s_a1, 1)
                    a.wait_ge(s_a1, bA + 3 * t + 1)
                    a.activation(ln_var[:, t:t + 1], ln_var[:, t:t + 1],
                                 AF.Ln, bias=c_lneps[:, 0:1], scale=1.0 / 1024
                                 ).then_inc(s_a1, 1)
                    a.wait_ge(s_a1, bA + 3 * t + 2)
                    a.activation(ln_rstd[:, t:t + 1], ln_var[:, t:t + 1],
                                 AF.Exp, scale=-0.5).then_inc(s_a1, 1)

            @blk.tensor
            def _(te):
                te.wait_ge(sem("tab0"), 1)
                for t in range(TT):
                    te.wait_ge(s_nm, bT + t + 1)
                    if t >= 2:
                        te.wait_ge(s_cp, bT + t - 1)
                    for ft in range(8):
                        ins = te.transpose(
                            ps_ln[t % 2][:, ft * 128:(ft + 1) * 128],
                            xns2v[:, t % 2, ft * 128:(ft + 1) * 128], ident[:, :])
                    ins.then_inc(s_tp, 1)
        if keep_reg:
            return s_cp, bT + TT, reg_ln
        reg_ln.close()
        return s_cp, bT + TT

    if PH >= 1:
        ln1_done = emit_ln("ln1", [(ldw, x_ready[0])])
    final_wait = ln1_done if PH >= 1 else (ldw, x_ready[0])

    AS = _A_STAGE
    if PH >= 2:
        reg_a = ExitStack()

        def sba(name, shape, dtype, reg=None):
            return (reg or reg_a).enter_context(nc.sbuf_tensor(name, shape, dtype))

        wgT = sba("wgT", [128, 8 * 512], BF16)
        woT = sba("woT", [128, 4 * 1024], BF16)
        qT = sba("qT", [128, 4 * 2048], BF16)
        wraw = qT
        qTv = qT[:, :].rearrange("p (o t) -> p o t", t=2048)
        num_sb = sba("num_sb", [128, TT * 520], BF16)
        numv = num_sb[:, :].rearrange("p (m f) -> p m f", f=520)
        hist = sba("hist", [128, (TT // 2) * 520], BF16)

        def histv(m):
            return hist[(m % 2) * 64:(m % 2) * 64 + 64,
                        (m // 2) * 520:(m // 2) * 520 + 520]

        Pst = sba("Pst", [64, 520], F32)
        ptmp = sba("ptmp", [64, 520], F32)
        lng_bc = sba("lng_bc", [128, 8], F32)
        decvec = sba("decvec", [128, 8], F32)
        gam1p = sba("gam1p", [128, 8], F32)
        g128mp = sba("g128mp", [64, 128], F32)
        coefc = sba("coefc", [64, 32], F32)
        dls = sba("dls", [1, 8], F32)
        dl_e = sba("dl_e", [1, 8], F32)
        dl_g = sba("dl_g", [1, 8], F32)
        dl_l = sba("dl_l", [1, 8], F32)
        coefs = sba("coefs", [1, 32], F32)
        expn_s = sba("expn_s", [1, 4], F32)
        flg_s = sba("flg_s", [1, 4], F32)
        iop1 = sba("iop1", [128, 1], F32)
        rev1 = sba("rev1", [128, 1], F32)
        bgbc = sba("bgbc", [128, 512], BF16)
        wgbr_s = sba("wgbr_s", [1, 512], F32)

        _xf = xns2[:, :].bitcast(F32)
        scrD = [_xf[:, 0:128], _xf[:, 128:256]]
        scrE = [_xf[:, 256:384], _xf[:, 384:512]]
        m0buf = [_xf[:, 0:512], _xf[:, 512:1024]]
        t3buf = [sqdump[:, 0:512], sqdump[:, 512:1024]]

        reg_s1 = ExitStack()
        kTh = sba("kTh", [128, 4 * 1024], BF16, reg_s1)
        vTh = sba("vTh", [128, 4 * 1024], BF16, reg_s1)
        kThv = kTh[:, :].rearrange("p (o t) -> p o t", t=1024)
        vThv = vTh[:, :].rearrange("p (o t) -> p o t", t=1024)
        dTf = sba("dTf", [128, 8 * 128], BF16, reg_s1)
        vaug = sba("vaug", [128, 520], BF16, reg_s1)
        vaugv = vaug[:, :].rearrange("p (h f) -> p h f", f=65)
        decrow = sba("decrow", [128, 512], BF16, reg_s1)
        growb = sba("growb", [64, 520], BF16, reg_s1)
        ktil = sba("ktil", [128, 512], BF16, reg_s1)
        abuf = [sba(f"abuf{i}", [128, 256], BF16, reg_s1) for i in range(2)]
        rowbuf = ktil[:, :].bitcast(F32)[0:1, 0:128]

        s_wtp = sem("a_wtp")
        s_wcp = sem("a_wcp")
        _wcnt = {"dma": 0, "grp": 0}

        def emit_aw(tag, specs, extra_waits=()):
            serialized = all(specs[i][4] is specs[i - 1][4]
                             for i in range(1, len(specs)))
            with nc.Block(tag) as blk:
                reg_awp = ExitStack()
                ps_w = [reg_awp.enter_context(
                    nc.psum_tensor(f"ps_{tag}{i}", [128, 1024], BF16))
                    for i in range(2)]
                g0 = _wcnt["grp"]
                s_wl = sem(f"awl_{tag}")

                @blk.gpsimd
                def _(g):
                    cum = g0
                    for w, (wd, wt, not_, nit, stg) in enumerate(specs):
                        if w >= 1 and stg is specs[w - 1][4]:
                            g.wait_ge(s_wtp, cum)
                        cum += not_
                        g.dma_start(
                            out=stg[:, 0:not_ * nit * 128].rearrange(
                                "p (o f) -> p o f", f=nit * 128),
                            in_=wd.rearrange("(o p) f -> p o f", p=128)
                        ).then_inc(s_wl, 16)

                @blk.tensor
                def _(te):
                    te.wait_ge(sem("tab0"), 1)
                    for s, n in extra_waits:
                        te.wait_ge(s, n)
                    gidx = g0
                    if not serialized:
                        te.wait_ge(s_wl, 16 * len(specs))
                    for w, (wd, wt, not_, nit, stg) in enumerate(specs):
                        if serialized:
                            te.wait_ge(s_wl, 16 * (w + 1))
                        wrv = stg[:, 0:not_ * nit * 128].rearrange(
                            "p (o f) -> p o f", f=nit * 128)
                        for ot in range(not_):
                            if gidx >= g0 + 2:
                                te.wait_ge(s_wcp, gidx - 1)
                            for it in range(nit):
                                ins = te.transpose(
                                    ps_w[gidx % 2][:, it * 128:(it + 1) * 128],
                                    wrv[:, ot, it * 128:(it + 1) * 128],
                                    ident[:, :])
                            ins.then_inc(s_wtp, 1)
                            gidx += 1

                @blk.vector
                def _(v):
                    gidx = g0
                    for w, (wd, wt, not_, nit, stg) in enumerate(specs):
                        wtv = wt[:, :].rearrange("p (i f) -> p i f", i=nit)
                        for ot in range(not_):
                            v.wait_ge(s_wtp, gidx + 1)
                            v.tensor_copy(
                                wtv[:, :, ot * 128:(ot + 1) * 128],
                                ps_w[gidx % 2][:, 0:nit * 128].rearrange(
                                    "p (i c) -> p i c", c=128)).then_inc(s_wcp, 1)
                            gidx += 1
                _wcnt["grp"] = g0 + sum(s[2] for s in specs)
                reg_awp.close()
            return _wcnt["grp"]

        reg_wq = ExitStack()
        wqT = sba("wqT", [128, 8 * 512], BF16, reg_wq)
        AW1_DONE = None

        AW1_DONE = 16
        if AS >= 0:
            emit_aw("aw1", [(wg_d, wgT, 4, 8, wraw),
                            (wo_d, woT, 8, 4, wraw),
                            (wq_d, wqT, 4, 8, wraw)],
                    extra_waits=[ln1_done])
            final_wait = (sem("a_wcp"), AW1_DONE)

        with nc.Block("atab") if AS >= 1 else _NullBlock() as blk:
            reg_tbp = ExitStack()
            ps_bc = reg_tbp.enter_context(nc.psum_tensor("ps_bc", [128, 512], F32))
            s_tl = sem("at_l")
            s_tt = sem("at_tt")
            OPS = []

            def op(eng, fn):
                OPS.append((eng, fn))

            op("a", lambda a: a.activation(dl_e[:, :], dls[:, :], AF.Exp, scale=-1.0))
            op("v", lambda v: v.tensor_scalar_add(dl_g[:, :], dl_e[:, :], 1.0))
            op("v", lambda v: v.reciprocal(dl_g[:, :], dl_g[:, :]))
            op("v", lambda v: v.tensor_scalar_max(dl_g[:, :], dl_g[:, :], 1e-37))
            op("a", lambda a: a.activation(dl_l[:, :], dl_g[:, :], AF.Ln))
            op("t", lambda t: t.matmul(ps_bc[:, 0:8], ones1f[:, :], dl_l[:, :],
                                       start=True, stop=True))
            op("v", lambda v: v.tensor_copy(lng_bc[:, :], ps_bc[:, 0:8]))
            for m in range(16):
                op("a", lambda a, m=m: a.activation(
                    rowbuf[:, m * 8:(m + 1) * 8], dl_l[:, :], AF.Exp, scale=128.0 * m))
            op("t", lambda t: t.matmul(ps_bc[0:64, 128:256], ones1f[:, 0:64],
                                       rowbuf[:, :], start=True, stop=True))
            op("v", lambda v: v.tensor_copy(g128mp[:, :], ps_bc[0:64, 128:256]))
            for r in range(4):
                op("a", lambda a, r=r: a.activation(
                    coefs[:, r * 8:(r + 1) * 8], dl_l[:, :], AF.Exp,
                    scale=expn_s[0:1, r:r + 1]))
            for r in range(4):
                op("v", lambda v, r=r: v.tensor_scalar_mul(
                    coefs[:, r * 8:(r + 1) * 8], coefs[:, r * 8:(r + 1) * 8],
                    flg_s[0:1, r:r + 1]))
            op("t", lambda t: t.matmul(ps_bc[0:64, 256:288], ones1f[:, 0:64],
                                       coefs[:, :], start=True, stop=True))
            op("v", lambda v: v.tensor_copy(coefc[:, :], ps_bc[0:64, 256:288]))
            op("v", lambda v: v.tensor_scalar_add(iop1[:, :], iotac[:, :], 1.0))
            op("v", lambda v: v.tensor_scalar(rev1[:, :], iotac[:, :], -1.0, 127.0,
                                              ALU.mult, ALU.add))
            op("v", lambda v: v.tensor_scalar_mul(scrD[0][:, 0:8], lng_bc[:, :],
                                                  iop1[:, 0:1]))
            op("v", lambda v: v.tensor_scalar_mul(scrD[1][:, 0:8], lng_bc[:, :],
                                                  rev1[:, 0:1]))
            op("a", lambda a: a.activation(gam1p[:, :], scrD[0][:, 0:8], AF.Exp))
            op("a", lambda a: a.activation(decvec[:, :], scrD[1][:, 0:8], AF.Exp))
            op("v", lambda v: v.memset(decrow[:, :], 1.0))
            for h in range(8):
                op("v", lambda v, h=h: v.tensor_scalar_mul(
                    decrow[:, h * 64:(h + 1) * 64],
                    decrow[:, h * 64:(h + 1) * 64], decvec[:, h:h + 1]))
            op("v", lambda v: v.memset(growb[:, :], 1.0))
            for h in range(8):
                op("v", lambda v, h=h: v.tensor_scalar_mul(
                    growb[:, h * 65:(h + 1) * 65],
                    growb[:, h * 65:(h + 1) * 65], g128mp[:, 8 + h:9 + h]))
            for h in range(8):
                op("v", lambda v, h=h: v.tensor_scalar(
                    scrD[h % 2][:, :], emat[:, :], lng_bc[:, h:h + 1], 0.0,
                    ALU.mult, ALU.min))
                op("a", lambda a, h=h: a.activation(scrE[h % 2][:, :],
                                                    scrD[h % 2][:, :], AF.Exp))
                op("v", lambda v, h=h: v.tensor_mul(
                    dTf[:, h * 128:(h + 1) * 128], scrE[h % 2][:, :], trif[:, :]))
            op("t", lambda t: t.matmul(ps_bc[:, 0:512], ones1f[:, :], wgbr_s[:, :],
                                       start=True, stop=True))
            op("v", lambda v: v.tensor_copy(bgbc[:, :], ps_bc[:, 0:512]))
            for h in range(8):
                op("v", lambda v, h=h: v.memset(
                    vaug[:, h * 65 + 64:h * 65 + 65], 1.0))
            TABN = len(OPS)

            @blk.sync
            def _(sync):
                sync.dma_start(out=dls[:, :], in_=dl_d[:, :]).then_inc(s_tl, 16)
                sync.dma_start(out=expn_s[:, :], in_=expn_d[:, :]).then_inc(s_tl, 16)
                sync.dma_start(out=flg_s[:, :], in_=flg_d[:, :]).then_inc(s_tl, 16)
                sync.dma_start(out=wgbr_s[:, :], in_=wgbr_d[:, :]).then_inc(s_tl, 16)

            def run_ops(eng, code):
                eng.wait_ge(s_tl, 64)
                eng.wait_ge(sem("tab0"), 1)
                eng.wait_ge(ln1_done[0], ln1_done[1])
                eng.wait_ge(sem("a_wcp"), 16)
                for k, (e, fn) in enumerate(OPS):
                    if e == code:
                        eng.wait_ge(s_tt, k)
                        fn(eng).then_inc(s_tt, 1)

            @blk.scalar
            def _(a):
                run_ops(a, "a")

            @blk.vector
            def _(v):
                run_ops(v, "v")

            @blk.tensor
            def _(te):
                run_ops(te, "t")
        reg_tbp.close()



        reg_pp = ExitStack()
        ps_p = [reg_pp.enter_context(
            nc.psum_tensor(f"ps_p{i}", [128, 512], F32)) for i in range(2)]
        s_pp = sem("ap_pp")
        s_dm = sem("ap_dm")
        s_ph = sem("ap_ph")
        s_p3 = sem("ap_p3")
        s_da = sem("ap_da")
        s_vc = sem("ap_vc")
        UKIND = []

        def _bank_wait(te, u):
            if u >= 2:
                k2, i2 = UKIND[u - 2]
                if k2 == "qk":
                    te.wait_ge(s_ph, 2 * (i2 + 1))
                else:
                    te.wait_ge(s_vc, i2 + 1)

        def emit_proj(tag, units, pe_waits):
            u0 = len(UKIND)
            j0 = sum(1 for k, i in UKIND if k == "qk")
            w0 = sum(1 for k, i in UKIND if k == "v")
            jj, ww = j0, w0
            for kind, _, _, _ in units:
                if kind == "qk":
                    UKIND.append(("qk", jj)); jj += 1
                else:
                    UKIND.append(("v", ww)); ww += 1

            with nc.Block(tag) as blk:
                @blk.tensor
                def _(te):
                    for s, n in pe_waits:
                        te.wait_ge(s, n)
                    for ui, (kind, wtv, ov, rhs) in enumerate(units):
                        u = u0 + ui
                        _bank_wait(te, u)
                        for it in range(8):
                            ins = te.matmul(ps_p[u % 2][:, :],
                                            wtv[:, it, :], rhs[:, it, :],
                                            start=(it == 0), stop=(it == 7))
                        ins.then_inc(s_pp, 1)

                @blk.vector
                def _(v):
                    qk_uis = [ui for ui, (k, _, _, _) in enumerate(units)
                              if k == "qk"]
                    for ui in qk_uis:
                        kind, wtv, ov, rhs = units[ui]
                        j = UKIND[u0 + ui][1]
                        v.wait_ge(s_p3, j + 1)
                        v.wait_ge(s_ph, 2 * j + 1)
                        v.tensor_add(ov, ov, t3buf[j % 2][:, :]).then_inc(s_da, 1)

                @blk.scalar
                def _(a):
                    prev = [None]

                    def t3_of(j):
                        a.wait_ge(s_ph, 2 * (j + 1))
                        if j >= 2:
                            a.wait_ge(s_da, j - 1)
                        a.activation(t3buf[j % 2][:, :], m0buf[j % 2][:, :],
                                     AF.Exp, scale=-1.0).then_inc(s_p3, 1)
                    for ui, (kind, wtv, ov, rhs) in enumerate(units):
                        u = u0 + ui
                        a.wait_ge(s_pp, u + 1)
                        if kind == "qk":
                            j = UKIND[u][1]
                            a.activation(ov, ps_p[u % 2][:, :],
                                         AF.Relu).then_inc(s_ph, 1)
                            if j >= 2:
                                a.wait_ge(s_p3, j - 1)
                            a.activation(m0buf[j % 2][:, :], ps_p[u % 2][:, :],
                                         AF.Relu, scale=-1.0).then_inc(s_ph, 1)
                            if prev[0] is not None:
                                t3_of(prev[0])
                            prev[0] = j
                        else:
                            a.activation(ov, ps_p[u % 2][:, :],
                                         AF.Identity).then_inc(s_vc, 1)
                    if prev[0] is not None:
                        t3_of(prev[0])

        def _wtv(wT):
            return wT[:, :].rearrange("p (i f) -> p i f", i=8)

        def _rhs(tc):
            return xnTv[:, :, tc * 512:(tc + 1) * 512]

        qunits = [("qk", _wtv(wqT)[:, :, ot * 128:(ot + 1) * 128],
                   qTv[:, ot, tc * 512:(tc + 1) * 512], _rhs(tc))
                  for ot in range(4) for tc in range(4)]
        if AS >= 2:
            emit_proj("qproj", qunits,
                      [(ln1_done[0], ln1_done[1]), (s_wcp, AW1_DONE),
                       (sem("at_tt"), TABN)])
            final_wait = (s_da, 16)
        reg_wq.close()

        reg_s2 = ExitStack()
        wkT = sba("wkT", [128, 8 * 512], BF16, reg_s2)
        wvT = sba("wvT", [128, 8 * 512], BF16, reg_s2)
        AW2_DONE = 24
        if AS >= 3:
            emit_aw("aw2", [(wk_d, wkT, 4, 8, kTh),
                            (wv_d, wvT, 4, 8, vTh)],
                    extra_waits=[(s_pp, 16)])
            final_wait = (s_wcp, AW2_DONE)

        reg_sc = ExitStack()
        ps_vt = reg_sc.enter_context(nc.psum_tensor("ps_vt", [128, 1024], BF16))
        ps_n = [reg_sc.enter_context(
            nc.psum_tensor(f"ps_n{i}", [128, 260], F32)) for i in range(2)]
        ps_s = [reg_sc.enter_context(
            nc.psum_tensor(f"ps_s{i}", [64, 260], F32)) for i in range(2)]
        ps_vtv = ps_vt[:, :].rearrange("p (s c) -> p s c", c=128)
        s_tp2 = sem("as_tp")
        s_dcp = sem("as_dcp")
        s_dk = sem("as_dk")
        s_pa = sem("as_pa")
        s_aa = sem("as_aa")
        s_pi = sem("as_pi")
        s_dse = sem("as_dse")
        s_dp1 = sem("as_dp1")
        s_dst = sem("as_dst")

        def emit_kvproj(half):
            units = []
            for ot in range(4):
                for tcl in range(2):
                    units.append(("qk", _wtv(wkT)[:, :, ot * 128:(ot + 1) * 128],
                                  kThv[:, ot, tcl * 512:(tcl + 1) * 512],
                                  _rhs(half * 2 + tcl)))
            for ot in range(4):
                for tcl in range(2):
                    units.append(("v", _wtv(wvT)[:, :, ot * 128:(ot + 1) * 128],
                                  vThv[:, ot, tcl * 512:(tcl + 1) * 512],
                                  _rhs(half * 2 + tcl)))
            waits = [(s_wcp, AW2_DONE)]
            if half == 1:
                waits.append((s_dst, 16))
                waits.append((s_aa, 32))
            emit_proj(f"kvproj{half}", units, waits)

        def emit_ascan(half):
            with nc.Block(f"ascan{half}") as blk:
                @blk.tensor
                def _(te):
                    te.wait_ge(s_da, 16 + 8 * (half + 1))
                    te.wait_ge(s_vc, 8 * (half + 1))
                    for ml in range(8):
                        m = half * 8 + ml
                        if m >= 1:
                            te.wait_ge(s_dcp, m)
                            te.wait_ge(s_dk, m)
                            te.wait_ge(s_dst, 2 * m)
                        for hp in range(4):
                            ins = te.transpose(ps_vtv[:, hp, :],
                                               vThv[:, hp, ml * 128:(ml + 1) * 128],
                                               ident[:, :])
                        for hp in range(4):
                            ins = te.transpose(ps_vtv[:, 4 + hp, :],
                                               kThv[:, hp, ml * 128:(ml + 1) * 128],
                                               ident[:, :])
                        ins.then_inc(s_tp2, 1)
                        def a_pair(hp):
                            gi = 4 * m + hp
                            if gi >= 1:
                                te.wait_ge(s_aa, gi)
                            te.matmul(ps_p[0][:, 0:128],
                                      kThv[0:64, hp, ml * 128:(ml + 1) * 128],
                                      qTv[0:64, hp, m * 128:(m + 1) * 128],
                                      start=True, stop=True, tile_position=(0, 0))
                            te.matmul(ps_p[1][:, 0:128],
                                      kThv[64:128, hp, ml * 128:(ml + 1) * 128],
                                      qTv[64:128, hp, m * 128:(m + 1) * 128],
                                      start=True, stop=True,
                                      tile_position=(64, 0)).then_inc(s_pa, 1)

                        def i_pair(hp):
                            te.wait_ge(s_aa, 4 * m + hp + 1)
                            last = None
                            for hh in range(2):
                                h = 2 * hp + hh
                                last = te.matmul(
                                    ps_n[h // 4][:, (h % 4) * 65:(h % 4) * 65 + 65],
                                    abuf[hp % 2][:, hh * 128:(hh + 1) * 128],
                                    vaugv[:, h, :],
                                    start=(h % 4 == 0), stop=(h % 4 == 3))
                            for hh in range(2):
                                h = 2 * hp + hh
                                last = te.matmul(
                                    ps_s[h // 4][:, (h % 4) * 65:(h % 4) * 65 + 65],
                                    ktil[:, h * 64:(h + 1) * 64],
                                    vaugv[:, h, :],
                                    start=(h % 4 == 0), stop=(h % 4 == 3))
                            last.then_inc(s_pi, 1)
                        te.wait_ge(s_dk, m + 1)
                        te.wait_ge(s_dcp, m + 1)
                        a_pair(0)
                        a_pair(1)
                        for hp in range(2, 4):
                            i_pair(hp - 2)
                            a_pair(hp)
                        i_pair(2)
                        i_pair(3)

                @blk.vector
                def _(v):
                    if half == 0:
                        v.memset(Pst[:, :], 0.0)
                    for ml in range(8):
                        m = half * 8 + ml
                        v.wait_ge(s_tp2, m + 1)
                        v.tensor_mul(ktil[:, :], ps_vt[:, 512:1024],
                                     decrow[:, :]).then_inc(s_dk, 1)
                        v.tensor_copy(
                            vaugv[:, 0:8, 0:64],
                            ps_vt[:, 0:512].rearrange("p (h f) -> p h f", f=64)
                        ).then_inc(s_dcp, 1)
                        for hp in range(4):
                            gi = 4 * m + hp
                            v.wait_ge(s_pa, gi + 1)
                            if gi >= 2:
                                v.wait_ge(s_pi, gi - 1)
                            v.tensor_mul(abuf[hp % 2][:, 0:128],
                                         ps_p[0][:, 0:128],
                                         dTf[:, (2 * hp) * 128:(2 * hp + 1) * 128])
                            v.tensor_mul(abuf[hp % 2][:, 128:256],
                                         ps_p[1][:, 0:128],
                                         dTf[:, (2 * hp + 1) * 128:(2 * hp + 2) * 128]
                                         ).then_inc(s_aa, 1)
                        v.wait_ge(s_pi, 4 * (m + 1))
                        v.tensor_copy(numv[:, m, 0:260], ps_n[0][:, :])
                        v.tensor_copy(numv[:, m, 260:520], ps_n[1][:, :])
                        v.tensor_copy(histv(m), Pst[:, :]).then_inc(s_dse, 1)
                        v.wait_ge(s_dse, m + 1)
                        v.tensor_mul(ptmp[:, :], Pst[:, :], growb[:, :]
                                     ).then_inc(s_dp1, 1)
                        v.wait_ge(s_dp1, m + 1)
                        v.tensor_add(Pst[:, 0:260], ptmp[:, 0:260],
                                     ps_s[0][:, :]).then_inc(s_dst, 1)
                        v.tensor_add(Pst[:, 260:520], ptmp[:, 260:520],
                                     ps_s[1][:, :]).then_inc(s_dst, 1)

        if AS >= 4:
            emit_kvproj(0)
            final_wait = (s_vc, 8)
        if AS >= 5:
            emit_ascan(0)
            final_wait = (s_dst, 16)
        if AS >= 6:
            emit_kvproj(1)
            final_wait = (s_vc, 16)
        if AS >= 7:
            emit_ascan(1)
            final_wait = (s_dst, 32)
        reg_sc.close()
        reg_s2.close()
        reg_pp.close()
        reg_s1.close()

        reg_cl = ExitStack()
        sag = sba("sag", [64, 520], F32, reg_cl)
        gm2 = sba("gm2", [64, 520], F32, reg_cl)
        gmbf = [sba(f"gmbf{i}", [128, 520], BF16, reg_cl) for i in range(2)]
        ctmp = sba("ctmp", [128, 520], F32, reg_cl)
        lbuf = sba("lbuf", [128, 2 * 520], F32, reg_cl)

        def lbv(r):
            return lbuf[(r % 2) * 64:(r % 2) * 64 + 64,
                        (r // 2) * 520:(r // 2) * 520 + 520]

        reg_cr = ExitStack()
        ps_c = [reg_cr.enter_context(
            nc.psum_tensor(f"ps_c{i}", [128, 260], F32)) for i in range(2)]
        s_g = sem("ac_g")
        s_cm = sem("ac_cm")
        s_cs = sem("ac_cs")
        s_cd = sem("ac_cd")

        with (nc.Block("aag") if AS >= 8 else _NullBlock()) as blk:
            s_ao = sem("ag_o")
            s_cc = sem("ag_cc")
            s_li = sem("ag_li")
            s_cmb = sem("ag_cmb")

            @blk.sync
            def _(sync):
                sync.wait_ge(s_dst, 2 * TT)
                sync.dma_start(out=aga_in[:, :], in_=Pst[:, :]).then_inc(s_ao, 16)

            @blk.gpsimd
            def _(g):
                g.wait_ge(s_ao, 16)
                g.collective_compute(
                    "AllGather", ALU.bypass, replica_groups=GROUPS,
                    ins=[aga_in[:, :]], outs=[aga_out[:, :]]).then_inc(s_cc, 1)
                g.wait_ge(s_cc, 1)
                g.dma_start(out=lbuf[:, :].rearrange("p (r2 f) -> p r2 f", f=520),
                            in_=aga_out.rearrange("(r2 r1 p) f -> (r1 p) r2 f",
                                                  r2=2, r1=2)).then_inc(s_li, 16)

        def corr_vec(v, m, mi, phase):
            if mi >= 2:
                v.wait_ge(s_cm, mi - 1)
            if phase == 0:
                src_ap = histv(m)
            else:
                for h in range(8):
                    ins = v.tensor_scalar_mul(
                        gm2[:, h * 65:(h + 1) * 65],
                        sag[:, h * 65:(h + 1) * 65],
                        g128mp[:, m * 8 + h:m * 8 + h + 1])
                ins.then_inc(s_g, 1)
                v.wait_ge(s_g, TT + 2 * m + 1)
                src_ap = gm2[:, :]
            v.tensor_copy(gmbf[mi % 2][0:64, :], src_ap)
            v.tensor_copy(gmbf[mi % 2][64:128, :], src_ap).then_inc(s_g, 1)
            v.wait_ge(s_cm, mi + 1)
            for h in range(8):
                ins = v.tensor_scalar_mul(
                    ctmp[:, h * 65:(h + 1) * 65],
                    ps_c[h % 2][:, (h // 2) * 65:(h // 2) * 65 + 65],
                    gam1p[:, h:h + 1])
            ins.then_inc(s_cs, 1)
            v.wait_ge(s_cs, mi + 1)
            v.tensor_add(numv[:, m, :], numv[:, m, :],
                         ctmp[:, :]).then_inc(s_cd, 1)

        def corr_te(te, m, mi, phase):
            te.wait_ge(s_g, m + 1 if phase == 0 else TT + 2 * m + 2)
            if mi >= 1:
                te.wait_ge(s_cs, mi)
            last = None
            for h in range(8):
                ot, half = h // 2, h % 2
                last = te.matmul(
                    ps_c[h % 2][:, (h // 2) * 65:(h // 2) * 65 + 65],
                    qTv[half * 64:half * 64 + 64, ot,
                        m * 128:(m + 1) * 128],
                    gmbf[mi % 2][half * 64:half * 64 + 64, h * 65:h * 65 + 65],
                    start=(h < 2), stop=(h >= 6))
            last.then_inc(s_cm, 1)

        with (nc.Block("acorr0") if AS >= 9 else _NullBlock()) as blk:
            @blk.vector
            def _(v):
                v.wait_ge(s_dst, 2 * TT)
                for m in range(TT):
                    corr_vec(v, m, m, 0)

            @blk.tensor
            def _(te):
                for m in range(TT):
                    corr_te(te, m, m, 0)

        with (nc.Block("aagc") if AS >= 8 else _NullBlock()) as blk:
            @blk.vector
            def _(v):
                v.memset(sag[:, :], 0.0)
                v.wait_ge(sem("ag_li"), 16)
                for r in range(4):
                    for h in range(8):
                        ins = v.tensor_scalar_mul(
                            ptmp[:, h * 65:(h + 1) * 65],
                            lbv(r)[:, h * 65:(h + 1) * 65],
                            coefc[:, r * 8 + h:r * 8 + h + 1])
                    ins.then_inc(sem("ag_cmb"), 1)
                    v.wait_ge(sem("ag_cmb"), 2 * r + 1)
                    v.tensor_add(sag[:, :], sag[:, :],
                                 ptmp[:, :]).then_inc(sem("ag_cmb"), 1)
                    v.wait_ge(sem("ag_cmb"), 2 * r + 2)

        with (nc.Block("acorr1") if AS >= 9 else _NullBlock()) as blk:
            @blk.vector
            def _(v):
                v.wait_ge(sem("ag_cmb"), 8)
                for m in range(TT):
                    corr_vec(v, m, TT + m, 1)

            @blk.tensor
            def _(te):
                for m in range(TT):
                    corr_te(te, m, TT + m, 1)
        reg_cl.close()

        reg_fn = ExitStack()
        ps_g = [reg_fn.enter_context(
            nc.psum_tensor(f"ps_g{i}", [128, 512], F32)) for i in range(2)]
        ps_y = [reg_fn.enter_context(
            nc.psum_tensor(f"ps_y{i}", [128, 512], BF16)) for i in range(2)]
        ps_o = [reg_fn.enter_context(
            nc.psum_tensor(f"ps_o{i}", [128, 512], F32)) for i in range(2)]
        ub = [sba(f"ub{i}", [128, 512], F32, reg_fn) for i in range(2)]
        eb = [sba(f"eb{i}", [128, 512], F32, reg_fn) for i in range(2)]
        gb = [sba(f"gb{i}", [128, 512], F32, reg_fn) for i in range(2)]
        den8 = [sba(f"den8{i}", [128, 8], F32, reg_fn) for i in range(2)]
        numf = [sba(f"numf{i}", [128, 520], F32, reg_fn) for i in range(2)]
        ybuf = [sba(f"ybuf{i}", [128, 512], BF16, reg_fn) for i in range(2)]
        yfm = [sba(f"yfm{i}", [128, 512], BF16, reg_fn) for i in range(2)]
        with (nc.Block("afin") if AS >= 10 else _NullBlock()) as blk:
            s_fg = sem("af_g")
            s_fu = sem("af_u")
            s_fe = sem("af_e")
            s_fv = sem("af_v")
            s_fw = sem("af_w")
            s_fy2 = sem("af_y2")
            s_fy = sem("af_y")
            s_fc = sem("af_c")
            s_fo = sem("af_o")
            s_fx = sem("af_x")

            @blk.tensor
            def _(te):
                te.wait_ge(sem("a_wcp"), 24)
                wgtv = wgT[:, :].rearrange("p (i f) -> p i f", i=8)
                wotv = woT[:, :].rearrange("p (i f) -> p i f", i=4)
                for m in range(TT):
                    te.wait_ge(sem("ac_cd"), TT + m + 1)
                    if m >= 2:
                        te.wait_ge(s_fu, m - 1)
                    for it in range(8):
                        ins = te.matmul(ps_g[m % 2][:, :],
                                        xnTv[:, it, m * 128:(m + 1) * 128],
                                        wgtv[:, it, 0:512],
                                        start=(it == 0), stop=(it == 7))
                    ins.then_inc(s_fg, 1)
                    te.wait_ge(s_fy2, m + 1)
                    if m >= 2:
                        te.wait_ge(s_fc, m - 1)
                    for ft in range(4):
                        ins = te.transpose(ps_y[m % 2][:, ft * 128:(ft + 1) * 128],
                                           ybuf[m % 2][:, ft * 128:(ft + 1) * 128],
                                           ident[:, :])
                    ins.then_inc(s_fy, 1)
                    te.wait_ge(s_fc, m + 1)
                    if m >= 1:
                        te.wait_ge(s_fx, 2 * m)
                    for dc in range(2):
                        for ft in range(4):
                            ins = te.matmul(ps_o[dc][:, :],
                                            yfm[m % 2][:, ft * 128:(ft + 1) * 128],
                                            wotv[:, ft, dc * 512:(dc + 1) * 512],
                                            start=(ft == 0), stop=(ft == 3))
                    ins.then_inc(s_fo, 1)

            @blk.vector
            def _(v):
                for m in range(TT):
                    v.wait_ge(s_fg, m + 1)
                    if m >= 2:
                        v.wait_ge(s_fe, m - 1)
                    v.tensor_add(ub[m % 2][:, :], ps_g[m % 2][:, :],
                                 bgbc[:, :]).then_inc(s_fu, 1)
                    v.wait_ge(s_fe, m + 1)
                    if m >= 2:
                        v.wait_ge(s_fy2, m - 1)
                    v.tensor_scalar_add(gb[m % 2][:, :], eb[m % 2][:, :],
                                        1.0).then_inc(s_fv, 1)
                    v.tensor_single_scalar(
                        den8[m % 2][:, :].rearrange("p (h o) -> p h o", o=1),
                        numv[:, m, :].rearrange("p (h f) -> p h f",
                                                f=65)[:, :, 64:65],
                        A_EPS, ALU.add).then_inc(s_fv, 1)
                    v.tensor_copy(numf[m % 2][:, :], numv[:, m, :]).then_inc(s_fv, 1)
                    v.wait_ge(s_fv, 3 * (m + 1))
                    v.reciprocal(gb[m % 2][:, :], gb[m % 2][:, :]).then_inc(s_fw, 1)
                    v.reciprocal(den8[m % 2][:, :],
                                 den8[m % 2][:, :]).then_inc(s_fw, 1)
                    v.wait_ge(s_fw, 2 * (m + 1))
                    for h in range(8):
                        ins = v.scalar_tensor_tensor(
                            ybuf[m % 2][:, h * 64:(h + 1) * 64],
                            numf[m % 2][:, h * 65:h * 65 + 64],
                            den8[m % 2][:, h:h + 1],
                            gb[m % 2][:, h * 64:(h + 1) * 64],
                            ALU.mult, ALU.mult)
                    ins.then_inc(s_fy2, 1)
                    v.wait_ge(s_fy, m + 1)
                    v.tensor_copy(yfm[m % 2][:, :],
                                  ps_y[m % 2][:, :]).then_inc(s_fc, 1)
                    v.wait_ge(s_fo, m + 1)
                    v.tensor_add(xv[:, m, 0:512], xv[:, m, 0:512],
                                 ps_o[0][:, :]).then_inc(s_fx, 1)
                    v.tensor_add(xv[:, m, 512:1024], xv[:, m, 512:1024],
                                 ps_o[1][:, :]).then_inc(s_fx, 1)

            @blk.scalar
            def _(a):
                for m in range(TT):
                    a.wait_ge(s_fu, m + 1)
                    if m >= 2:
                        a.wait_ge(s_fv, 3 * (m - 2) + 1)
                    a.activation(eb[m % 2][:, :], ub[m % 2][:, :], AF.Exp,
                                 scale=-1.0).then_inc(s_fe, 1)
        reg_fn.close()
        if AS >= 9:
            reg_cr.close()
        if AS >= 8:
            final_wait = (sem("ag_cmb"), 8)
        if AS >= 9:
            final_wait = (sem("ac_cd"), 2 * TT)
        if AS >= 10:
            final_wait = (sem("af_x"), 2 * TT)
        reg_a.close()

    prep_cnt = [0]

    def emit_ffn_prep(tag, gate):
        i = prep_cnt[0]
        prep_cnt[0] += 1
        reg_w = ExitStack()
        Cv = reg_w.enter_context(nc.sbuf_tensor(f"fp{i}_C", [128, 8 * 1024], BF16))
        Sv = reg_w.enter_context(nc.sbuf_tensor(f"fp{i}_S", [128, 8 * 1024], BF16))
        Cvv = Cv[:, :].rearrange("p (i f) -> p i f", i=8)
        Svv = Sv[:, :].rearrange("p (i f) -> p i f", i=8)
        s_pl = [sem(f"fp_l{j}") for j in range(4)]
        s_pa = sem("fp_a")
        s_ps = sem("fp_s")
        bL, b16 = i * 128, i * 16
        reg_fp = ExitStack()
        stA = [reg_fp.enter_context(nc.sbuf_tensor(f"fp{i}A{j}", [128, 1024], F32))
               for j in range(4)]
        stB = [reg_fp.enter_context(nc.sbuf_tensor(f"fp{i}B{j}", [128, 1024], F32))
               for j in range(4)]
        specs = []
        for ft in range(8):
            specs.append((th_d, thn_d, ft, Cvv, c_pihalf))
        for ft in range(8):
            specs.append((ph_d, phn_d, ft, Svv, None))
        with nc.Block(tag) as blk:
            @blk.sync
            def _(sync):
                sync.wait_ge(*gate)
                for u, (bd, nd, ft, ov, bias) in enumerate(specs):
                    if u >= 4:
                        sync.wait_ge(s_ps, b16 + u - 3)
                    sync.dma_start(out=stA[u % 4][:, :],
                                   in_=bd[ft * 128:(ft + 1) * 128, :]
                                   ).then_inc(s_pl[u % 4], 16)
                    sync.dma_start(out=stB[u % 4][:, :],
                                   in_=nd[ft * 128:(ft + 1) * 128, :]
                                   ).then_inc(s_pl[u % 4], 16)

            @blk.gpsimd
            def _(g):
                for u, (bd, nd, ft, ov, bias) in enumerate(specs):
                    g.wait_ge(s_pl[u % 4], bL + 32 * (u // 4 + 1))
                    g.tensor_add(stA[u % 4][:, :], stA[u % 4][:, :],
                                 stB[u % 4][:, :]).then_inc(s_pa, 1)

            @blk.scalar
            def _(a):
                for u, (bd, nd, ft, ov, bias) in enumerate(specs):
                    a.wait_ge(s_pa, b16 + u + 1)
                    if bias is None:
                        ins = a.activation(ov[:, ft, :], stA[u % 4][:, :], AF.Sin)
                    else:
                        ins = a.activation(ov[:, ft, :], stA[u % 4][:, :], AF.Sin,
                                           bias=bias[:, 0:1])
                    ins.then_inc(s_ps, 1)
        reg_fp.close()
        return (Cvv, Svv, (s_ps, b16 + 16), reg_w)

    ffn_cnt = [0]

    def emit_ffn(tag, ln_done, prep, extra_te_waits=(), ln_fuse=False):
        Cvv, Svv, prep_done, _ = prep
        ln_base = ln_done[1] - TT
        i = ffn_cnt[0]
        ffn_cnt[0] += 1
        s_fm = sem("ff_m")
        s_fsq = sem("ff_sq")
        s_fad = sem("ff_ad")
        s_fsr = sem("ff_sr")
        s_fxv = sem("ff_xv")
        bU = i * 32
        if ln_fuse:
            iL = ln_count[0]
            ln_count[0] += 1
            sL_v1, sL_a1 = sem("ln_v1"), sem("ln_a1")
            sL_nm, sL_tp, sL_cp = sem("ln_nm"), sem("ln_tp"), sem("ln_cp")
            bVL, bAL, bTL = iL * 2 * TT, iL * 3 * TT, iL * TT
            xns2vL = xns2[:, :].rearrange("p (b f) -> p b f", f=1024)
            reg_lnF = ExitStack()
            ps_lnF = [reg_lnF.enter_context(
                nc.psum_tensor(f"ps_ln{iL}_{k}", [128, 1024], BF16))
                for k in range(2)]

            def L_stats(v, t):
                v.reduce_sum(ln_sums[:, t:t + 1], xv[:, t, :], axis=AXX
                             ).then_inc(sL_v1, 1)
                v.wait_ge(sL_v1, bVL + 2 * t + 1)
                v.tensor_scalar_mul(ln_mneg[:, t:t + 1],
                                    ln_sums[:, t:t + 1], -1.0 / 1024
                                    ).then_inc(sL_v1, 1)

            def L_norm(v, t):
                v.wait_ge(sL_a1, bAL + 3 * t + 3)
                if t >= 2:
                    v.wait_ge(sL_tp, bTL + t - 1)
                v.tensor_scalar(xns2vL[:, t % 2, :], xv[:, t, :],
                                ln_mneg[:, t:t + 1], ln_rstd[:, t:t + 1],
                                ALU.add, ALU.mult).then_inc(sL_nm, 1)

            s_agiF = [sem(f"b_agi{c}") for c in range(4)]

            def L_copy(a, t):
                a.wait_ge(sL_tp, bTL + t + 1)
                a.copy(xnTv[:, :, t * 128:(t + 1) * 128],
                       ps_lnF[t % 2][:, :].rearrange(
                           "p (ft c) -> p ft c", c=128)
                       ).then_inc(sL_cp, 1)

            def L_stage(a, c):
                a.dma_start(
                    out=ag1_in[1024 * c:1024 * (c + 1), :].rearrange(
                        "(ft p) t -> p ft t", p=128),
                    in_=xnTv[:, :, 512 * c:512 * (c + 1)]
                ).then_inc(s_agiF[c], 16)

            def L_scalar(a, t):
                a.wait_ge(sL_v1, bVL + 2 * t + 2)
                a.activation(sqdump[:, :], xv[:, t, :], AF.Square,
                             bias=ln_mneg[:, t:t + 1], scale=1.0,
                             accum_out=ln_var[:, t:t + 1]).then_inc(sL_a1, 1)
                a.wait_ge(sL_a1, bAL + 3 * t + 1)
                a.activation(ln_var[:, t:t + 1], ln_var[:, t:t + 1],
                             AF.Ln, bias=c_lneps[:, 0:1], scale=1.0 / 1024
                             ).then_inc(sL_a1, 1)
                a.wait_ge(sL_a1, bAL + 3 * t + 2)
                a.activation(ln_rstd[:, t:t + 1], ln_var[:, t:t + 1],
                             AF.Exp, scale=-0.5).then_inc(sL_a1, 1)

            def L_transpose(te, t):
                te.wait_ge(sL_nm, bTL + t + 1)
                if t >= 2:
                    te.wait_ge(sL_cp, bTL + t - 1)
                for ft in range(8):
                    ins = te.transpose(
                        ps_lnF[t % 2][:, ft * 128:(ft + 1) * 128],
                        xns2vL[:, t % 2, ft * 128:(ft + 1) * 128], ident[:, :])
                ins.then_inc(sL_tp, 1)

        reg_ff = ExitStack()
        ps_r = [reg_ff.enter_context(
            nc.psum_tensor(f"ff{i}r{j}", [128, 512], F32)) for j in range(2)]
        ps_i = [reg_ff.enter_context(
            nc.psum_tensor(f"ff{i}i{j}", [128, 512], F32)) for j in range(2)]
        sq_r = [reg_ff.enter_context(
            nc.sbuf_tensor(f"ff{i}qr{j}", [128, 512], F32)) for j in range(2)]
        sq_i = [reg_ff.enter_context(
            nc.sbuf_tensor(f"ff{i}qi{j}", [128, 512], F32)) for j in range(2)]
        ssum = [reg_ff.enter_context(
            nc.sbuf_tensor(f"ff{i}ss{j}", [128, 512], F32)) for j in range(2)]
        rt = [reg_ff.enter_context(
            nc.sbuf_tensor(f"ff{i}rt{j}", [128, 512], F32)) for j in range(2)]
        NU = 32

        with nc.Block(tag) as blk:
            @blk.tensor
            def _(te):
                for s, n in extra_te_waits:
                    te.wait_ge(s, n)
                te.wait_ge(prep_done[0], prep_done[1] - 8)
                for u in range(NU):
                    m, o = u // 2, u % 2
                    te.wait_ge(ln_done[0], ln_base + m + 1)
                    if u >= 2:
                        te.wait_ge(s_fsq, 2 * bU + 2 * u - 2)
                    for it in range(8):
                        te.matmul(ps_r[u % 2][:, :], xnTv[:, it, m * 128:(m + 1) * 128],
                                  Cvv[:, it, o * 512:(o + 1) * 512],
                                  start=(it == 0), stop=(it == 7))
                    if u == 0:
                        te.wait_ge(*prep_done)
                    for it in range(8):
                        ins = te.matmul(ps_i[u % 2][:, :],
                                        xnTv[:, it, m * 128:(m + 1) * 128],
                                        Svv[:, it, o * 512:(o + 1) * 512],
                                        start=(it == 0), stop=(it == 7))
                    ins.then_inc(s_fm, 1)
                    if ln_fuse and u % 2 == 0 and u >= 6:
                        L_transpose(te, u // 2 - 3)
                if ln_fuse:
                    for t in range(TT - 3, TT):
                        L_transpose(te, t)

            @blk.scalar
            def _(a):
                for u in range(NU):
                    a.wait_ge(s_fm, bU + u + 1)
                    if u >= 2:
                        a.wait_ge(s_fad, bU + u - 1)
                    a.activation(sq_r[u % 2][:, :], ps_r[u % 2][:, :],
                                 AF.Square).then_inc(s_fsq, 1)
                    a.activation(sq_i[u % 2][:, :], ps_i[u % 2][:, :],
                                 AF.Square).then_inc(s_fsq, 1)
                    if u >= 1:
                        a.wait_ge(s_fad, bU + u)
                        if u >= 2:
                            a.wait_ge(s_fxv, bU + u - 1)
                        a.activation(rt[(u - 1) % 2][:, :], ssum[(u - 1) % 2][:, :],
                                     AF.Sqrt).then_inc(s_fsr, 1)
                    if ln_fuse and u % 2 == 0 and u >= 2:
                        L_scalar(a, u // 2 - 1)
                        if u >= 6:
                            L_copy(a, u // 2 - 3)
                        if u >= 12 and (u - 12) % 8 == 0:
                            L_stage(a, (u - 12) // 8)
                a.wait_ge(s_fad, bU + NU)
                a.wait_ge(s_fxv, bU + NU - 1)
                a.activation(rt[(NU - 1) % 2][:, :], ssum[(NU - 1) % 2][:, :],
                             AF.Sqrt).then_inc(s_fsr, 1)
                if ln_fuse:
                    L_scalar(a, TT - 1)
                    for t in range(TT - 3, TT):
                        L_copy(a, t)
                    L_stage(a, 3)

            @blk.vector
            def _(v):
                for u in range(NU):
                    m, o = u // 2, u % 2
                    v.wait_ge(s_fsq, 2 * bU + 2 * u + 2)
                    if u >= 2:
                        v.wait_ge(s_fsr, bU + u - 1)
                    v.tensor_add(ssum[u % 2][:, :], sq_r[u % 2][:, :],
                                 sq_i[u % 2][:, :]).then_inc(s_fad, 1)
                    if u >= 1:
                        v.wait_ge(s_fsr, bU + u)
                        m1, o1 = (u - 1) // 2, (u - 1) % 2
                        v.tensor_add(xv[:, m1, o1 * 512:(o1 + 1) * 512],
                                     xv[:, m1, o1 * 512:(o1 + 1) * 512],
                                     rt[(u - 1) % 2][:, :]).then_inc(s_fxv, 1)
                    if ln_fuse and u % 2 == 0 and u >= 2:
                        L_stats(v, u // 2 - 1)
                        if u >= 4:
                            L_norm(v, u // 2 - 2)
                v.wait_ge(s_fsr, bU + NU)
                v.tensor_add(xv[:, 15, 512:1024], xv[:, 15, 512:1024],
                             rt[(NU - 1) % 2][:, :]).then_inc(s_fxv, 1)
                if ln_fuse:
                    L_stats(v, TT - 1)
                    L_norm(v, TT - 2)
                    L_norm(v, TT - 1)
        reg_ff.close()
        if ln_fuse:
            ffn_ln3[0] = (sL_cp, bTL + TT)
            reg_lnF.close()
        return (s_fxv, bU + NU)

    ffn_ln3 = [None]

    if PH >= 3:
        FS = _F_STAGE
        prep1 = emit_ffn_prep("fprep1", final_wait)
        afx = final_wait[0]
        if FS >= 1:
            ln2_done = emit_ln("ln2", [],
                               tile_waits=lambda t: [(afx, 2 * t + 2)],
                               keep_reg=True)
            final_wait = ln2_done[:2]
        if FS >= 2:
            ffn2_done = emit_ffn("ffn2", ln2_done[:2], prep1,
                                 ln_fuse=(PH >= 4))
            final_wait = ffn2_done
            ln2_done[2].close()
        prep1[3].close()
        if PH >= 4 and FS >= 2:
            with nc.Block("bag1e") as blk:
                @blk.gpsimd
                def _(g_):
                    for c in range(4):
                        g_.wait_ge(sem(f"b_agi{c}"), 16)
                        g_.collective_compute(
                            "AllGather", ALU.bypass, replica_groups=GROUPS,
                            ins=[ag1_in[1024 * c:1024 * (c + 1), :]],
                            outs=[ag1_out[4096 * c:4096 * (c + 1), :]],
                        ).then_inc(sem(f"b_agc{c}"), 1)

    nw = {}

    def alloc_bwq():
        reg_bW = ExitStack()
        nw["reg"] = reg_bW

        def sbW(name, shape, dtype):
            return reg_bW.enter_context(nc.sbuf_tensor(name, shape, dtype))
        nw["wqv"] = sbW("wqv", [128, 8 * 768], BF16)
        nw["wqvv"] = nw["wqv"][:, :].rearrange("p (i c) -> p i c", i=8)
        nw["bq6"] = sbW("bq6", [128, 6], F32)
        nw["bqr_s"] = sbW("bqr_s", [1, 256], F32)
        nw["bpr_s"] = sbW("bpr_s", [1, 1024], F32)
        nw["gixs"] = sbW("gixs", [128, 8], mybir.dt.int32)
        nw["onesbf"] = sbW("onesbf", [1, 128], BF16)
        nw["bprbf"] = sbW("bprbf", [1, 1024], BF16)
        nw["bqvbf"] = sbW("bqvbf", [1, 256], BF16)
        nw["wpv"] = sbW("wpv", [128, 8 * 1024], BF16)
        nw["wpvv"] = nw["wpv"][:, :].rearrange("p (i f) -> p i f", i=8)
        nw["xg"] = [sbW(f"bxg{i}", [128, 8 * 512], BF16) for i in range(2)]

    def emit_bwq(gate):
        stg = nw["stg"]
        NS = len(stg)
        s_bwl = [sem(f"b_wl{k}") for k in range(NS)]
        s_bwb = sem("b_wb")
        s_bwc = sem("b_wc")
        wqvv, wpvv = nw["wqvv"], nw["wpvv"]
        tiles_q = [(wqkv_d, 128 * (i // 3), 256 * (i % 3),
                    wqvv[:, i // 3, 256 * (i % 3):256 * (i % 3) + 256])
                   for i in range(24)]
        tiles_p = [(wproj_d, 128 * (i // 4), 256 * (i % 4),
                    wpvv[:, i // 4, 256 * (i % 4):256 * (i % 4) + 256])
                   for i in range(32)]
        def _bwq_loader(eng, q):
            eng.wait_ge(*gate)
            for i, (srcd, r0, c0, dstv) in enumerate(tiles_q + tiles_p):
                if i % 2 != q:
                    continue
                s = i % NS
                if i >= NS:
                    eng.wait_ge(s_bwc, i - NS + 4)
                eng.dma_start(out=stg[s],
                              in_=srcd[r0:r0 + 128, c0:c0 + 256]
                              ).then_inc(s_bwl[s], 16)

        with nc.Block("bwq") as blk:
            @blk.scalar
            def _(adm):
                adm.wait_ge(*gate)
                adm.dma_start(out=nw["bq6"][:, :], in_=bqkv_d[:, :]
                              ).then_inc(s_bwb, 16)
                adm.dma_start(out=nw["bqr_s"][:, :], in_=bqvr_d[:, :]
                              ).then_inc(s_bwb, 16)
                adm.dma_start(out=nw["bpr_s"][:, :], in_=bpr_d[:, :]
                              ).then_inc(s_bwb, 16)
                adm.dma_start(out=nw["gixs"][:, :], in_=gix_d[:, :]
                              ).then_inc(s_bwb, 16)
                nl2 = [0] * NS
                for i, (srcd, r0, c0, dstv) in enumerate(tiles_q + tiles_p):
                    s = i % NS
                    if i % 2 == 0:
                        if i >= NS:
                            adm.wait_ge(s_bwc, i - NS + 4)
                        adm.dma_start(out=stg[s],
                                      in_=srcd[r0:r0 + 128, c0:c0 + 256]
                                      ).then_inc(s_bwl[s], 16)
                    if i == 0:
                        adm.wait_ge(s_bwc, 3)
                    nl2[s] += 1
                    adm.wait_ge(s_bwl[s], 16 * nl2[s])
                    adm.copy(dstv, stg[s]).then_inc(s_bwc, 1)

            @blk.sync
            def _(sy):
                _bwq_loader(sy, 1)

            @blk.vector
            def _(g_):
                g_.wait_ge(s_bwb, 64)
                g_.tensor_copy(nw["bprbf"][:, :], nw["bpr_s"][:, :]
                               ).then_inc(s_bwc, 1)
                g_.tensor_copy(nw["bqvbf"][:, :], nw["bqr_s"][:, :]
                               ).then_inc(s_bwc, 1)
                g_.memset(nw["onesbf"][:, :], 1.0).then_inc(s_bwc, 1)

    def emit_attn_b(ln3_done, gate_w):
        alloc_bwq()
        BS = _B_STAGE
        s_cp3 = ln3_done[0]
        cp3_base = ln3_done[1] - TT
        kTv = xnT[:, :].rearrange("p (j t) -> p j t", t=8192)

        reg_bP = ExitStack()

        def sbP(name, shape, dtype, reg):
            return reg.enter_context(nc.sbuf_tensor(name, shape, dtype))
        bq6 = nw["bq6"]
        gixs = nw["gixs"]
        onesbf = nw["onesbf"]
        wpvv = nw["wpvv"]
        wqvv = nw["wqvv"]
        bprbf = nw["bprbf"]
        bqvbf = nw["bqvbf"]
        negMp = sbP("negMp", [128, 2], F32, reg_bP)

        reg_bB = ExitStack()
        vtok = sbP("vtok", [128, 64 * 260], BF16, reg_bB)
        vtokv = vtok[:, :].rearrange("p (k f) -> p k f", f=260)
        qbuf = [sbP(f"qbuf{i}", [128, 2 * 512], BF16, reg_bB) for i in range(2)]
        qbv = [q[:, :].rearrange("p (j t) -> p j t", t=512) for q in qbuf]
        NE = 3
        E2 = [sbP(f"E2{i}", [128, 1024], BF16, reg_bB) for i in range(NE)]
        ctxb = sbP("ctxb", [128, 4 * 256], BF16, reg_bB)
        ctxbv = ctxb[:, :].rearrange("p (q f) -> p q f", f=256)
        ctxT = sbP("ctxT", [128, 2 * 512], BF16, reg_bB)
        ctxTv = ctxT[:, :].rearrange("p (fh t) -> p fh t", t=512)
        dtmp = sbP("dtmp", [128, 8], F32, reg_bB)

        reg_bA = ExitStack()
        xg = nw["xg"]
        xgv = [x_[:, :].rearrange("p (i t) -> p i t", i=8) for x_ in xg]
        qstage = sbP("bqs", [128, 2 * 512], BF16, reg_bA)
        qstgv = qstage[:, :].rearrange("p (j t) -> p j t", t=512)
        reg_bN = ExitStack()
        ps_m = reg_bN.enter_context(nc.psum_tensor("ps_m", [128, 256], F32))
        qmxa = sbP("qmxa", [128, 32], F32, reg_bN)
        kmxa = sbP("kmxa", [128, 32], F32, reg_bN)
        qmx2 = sbP("qmx2", [128, 2], F32, reg_bN)
        kmx2 = sbP("kmx2", [128, 2], F32, reg_bN)
        qsq2 = sbP("qsq2", [128, 2], F32, reg_bN)
        ksq2 = sbP("ksq2", [128, 2], F32, reg_bN)
        nrow = sbP("nrow", [1, 8], F32, reg_bN)

        reg_pj = ExitStack()
        ps_qk = [reg_pj.enter_context(nc.psum_tensor(f"ps_bqk{i}", [128, 512], F32))
                 for i in range(2)]
        ps_v = [reg_pj.enter_context(nc.psum_tensor(f"ps_bv{i}", [128, 256], F32))
                for i in range(2)]

        s_agi = [sem(f"b_agi{c}") for c in range(4)]
        s_agc = [sem(f"b_agc{c}") for c in range(4)]
        s_bwc = sem("b_wc")
        s_xgl = [sem(f"b_xgl{i}") for i in range(2)]
        s_pj = sem("b_pj")
        s_ev = sem("b_ev")
        s_qso = sem("b_qso")
        s_bnm = sem("b_nm")
        s_bnc = sem("b_nc")
        s_qbl = [sem(f"b_qbl{i}") for i in range(2)]
        s_bs2 = sem("b_s2")
        s_bex = sem("b_ex")
        s_bmk = sem("b_mk")
        s_bpv = sem("b_bpv")
        s_dnr = sem("b_dnr")
        s_bcx = sem("b_cx")
        s_ttc = sem("b_ttc")
        s_a2o = sem("b_a2o")
        s_a2c = [sem(f"b_a2c{d}") for d in range(4)]
        s_cgl = sem("b_cgl")
        s_bom = sem("b_om")
        s_box = sem("b_ox")

        units = []
        mask_thru = []
        pair_last = {}
        uof = {}
        mc = 0
        for g in range(16):
            for P in range(2):
                for j in range(4 * g + 3, -1, -1):
                    uu = len(units)
                    uof[(g, P, j)] = uu
                    if j == 0:
                        pair_last[(g, P)] = uu
                    if j >= 4 * g:
                        mc += 2
                    units.append((g, P, j))
                    mask_thru.append(mc)
        NU_B = len(units)


        nw["stg"] = [e[:, :].bitcast(F32)[:, 256 * h:256 * (h + 1)]
                     for e in E2 for h in range(2)]
        emit_bwq(gate_w)

        def _bp_tid(u):
            return 4 * (u % 4) + u // 4

        with nc.Block("bproj") if BS >= 2 else _NullBlock() as blk:
            @blk.sync
            def _(sync):
                for u in range(16):
                    r, c = u % 4, u // 4
                    sync.wait_ge(s_agc[c], 1)
                    if u >= 2:
                        sync.wait_ge(s_xgl[u % 2], 16 * (u // 2))
                        sync.wait_ge(s_pj, 8 * (u - 1))
                    sync.dma_start(
                        out=xgv[u % 2],
                        in_=ag1_out[4096 * c + 1024 * r:4096 * c + 1024 * (r + 1),
                                    :].rearrange("(ft p) tk -> p ft tk", p=128)
                    ).then_inc(s_xgl[u % 2], 16)

            @blk.tensor
            def _(te):
                te.wait_ge(s_cp3, cp3_base + 16)
                te.wait_ge(s_bwc, 27)
                for u in range(16):
                    te.wait_ge(s_xgl[u % 2], 16 * (u // 2 + 1))
                    for j in range(4):
                        if u > 0 or j >= 2:
                            guard = (8 * u + j - 1) if j >= 2 else (8 * u + j - 5)
                            te.wait_ge(s_ev, guard)
                        for it in range(8):
                            ins = te.matmul(ps_qk[j % 2][:, :],
                                            wqvv[:, it, j * 128:(j + 1) * 128],
                                            xgv[u % 2][:, it, :],
                                            start=(it == 0), stop=(it == 7))
                        ins.then_inc(s_pj, 1)
                    for i in range(4):
                        if u > 0 or i >= 2:
                            guard = (8 * u + i + 3) if i >= 2 else (8 * u + i - 1)
                            te.wait_ge(s_ev, guard)
                        for it in range(8):
                            te.matmul(ps_v[i % 2][:, :],
                                      xgv[u % 2][:, it, i * 128:(i + 1) * 128],
                                      wqvv[:, it, 512:768],
                                      start=(it == 0), stop=False)
                        te.matmul(ps_v[i % 2][:, :], onesbf[0:1, 0:128],
                                  bqvbf[0:1, :], start=False, stop=True
                                  ).then_inc(s_pj, 1)


            @blk.vector
            def _(v):
                for h in range(4):
                    v.memset(vtokv[:, :, h * 65 + 64:h * 65 + 65], 1.0)
                for u in range(16):
                    t = _bp_tid(u)
                    for j in range(2):
                        v.wait_ge(s_pj, 8 * u + j + 1)
                        if u >= 1:
                            v.wait_ge(s_qso, 32 * u)
                        v.tensor_scalar(qstgv[:, j, :], ps_qk[j % 2][:, :],
                                        bq6[:, j:j + 1], 0.125, ALU.add, ALU.mult
                                        ).then_inc(s_ev, 1)
                    for j in range(2, 4):
                        v.wait_ge(s_pj, 8 * u + j + 1)
                        if u == 0 and j == 2:
                            for c in range(4):
                                v.wait_ge(s_agi[c], 16)
                        v.tensor_scalar_add(kTv[:, j - 2, 512 * t:512 * (t + 1)],
                                            ps_qk[j % 2][:, :], bq6[:, j:j + 1]
                                            ).then_inc(s_ev, 1)
                    for i in range(4):
                        v.wait_ge(s_pj, 8 * u + 4 + i + 1)
                        for h in range(4):
                            ins = v.tensor_copy(
                                vtokv[:, 4 * t + i, h * 65:h * 65 + 64],
                                ps_v[i % 2][:, h * 64:(h + 1) * 64])
                        ins.then_inc(s_ev, 1)
                    v.wait_ge(s_ev, 8 * u + 4)
                    for j in range(2):
                        v.reduce_max(qmxa[:, j * 16 + t:j * 16 + t + 1],
                                     qstgv[:, j, :], axis=AXX,
                                     apply_absolute_value=True)
                    for j in range(2):
                        ins = v.reduce_max(kmxa[:, j * 16 + t:j * 16 + t + 1],
                                           kTv[:, j, 512 * t:512 * (t + 1)],
                                           axis=AXX, apply_absolute_value=True)
                    ins.then_inc(s_bnm, 1)

            @blk.gpsimd
            def _(g_):
                for u in range(16):
                    t = _bp_tid(u)
                    g_.wait_ge(s_ev, 8 * u + 2)
                    for jj in range(2):
                        g_.dma_start(
                            out=q_scr[128 * jj:128 * (jj + 1),
                                      512 * t:512 * (t + 1)],
                            in_=qstgv[:, jj, :]).then_inc(s_qso, 16)

        NOPS = [0]
        with nc.Block("bnrm") if BS >= 3 else _NullBlock() as blk:
            OPS = []

            def op(eng, fn):
                OPS.append((eng, fn))
            op("v", lambda v: v.reduce_max(qmx2[:, 0:1], qmxa[:, 0:16], axis=AXX))
            op("v", lambda v: v.reduce_max(qmx2[:, 1:2], qmxa[:, 16:32], axis=AXX))
            op("v", lambda v: v.reduce_max(kmx2[:, 0:1], kmxa[:, 0:16], axis=AXX))
            op("v", lambda v: v.reduce_max(kmx2[:, 1:2], kmxa[:, 16:32], axis=AXX))
            op("a", lambda a: a.activation(qsq2[:, :], qmx2[:, :], AF.Square))
            op("a", lambda a: a.activation(ksq2[:, :], kmx2[:, :], AF.Square))
            for j in range(2):
                for h01 in range(2):
                    op("t", lambda t, j=j, h01=h01: t.matmul(
                        ps_m[0:1, 2 * j + h01:2 * j + h01 + 1],
                        qsq2[h01 * 64:h01 * 64 + 64, j:j + 1],
                        onesc[h01 * 64:h01 * 64 + 64, 0:1],
                        start=True, stop=True))
                    op("t", lambda t, j=j, h01=h01: t.matmul(
                        ps_m[0:1, 4 + 2 * j + h01:4 + 2 * j + h01 + 1],
                        ksq2[h01 * 64:h01 * 64 + 64, j:j + 1],
                        onesc[h01 * 64:h01 * 64 + 64, 0:1],
                        start=True, stop=True))
            op("v", lambda v: v.tensor_copy(nrow[:, :], ps_m[0:1, 0:8]))
            op("v", lambda v: v.tensor_mul(nrow[0:1, 0:4], nrow[0:1, 0:4],
                                           nrow[0:1, 4:8]))
            op("a", lambda a: a.activation(nrow[0:1, 0:4], nrow[0:1, 0:4],
                                           AF.Sqrt))
            op("v", lambda v: v.tensor_scalar_mul(nrow[0:1, 0:4],
                                                  nrow[0:1, 0:4], -1.0))
            op("v", lambda v: v.tensor_tensor(nrow[0:1, 4:5], nrow[0:1, 0:1],
                                              nrow[0:1, 1:2], ALU.min))
            op("v", lambda v: v.tensor_tensor(nrow[0:1, 5:6], nrow[0:1, 2:3],
                                              nrow[0:1, 3:4], ALU.min))
            for P in range(2):
                op("t", lambda t, P=P: t.matmul(
                    ps_m[:, 8 + P:9 + P], ones1f[0:1, :], nrow[0:1, 4 + P:5 + P],
                    start=True, stop=True))
            for P in range(2):
                op("v", lambda v, P=P: v.tensor_copy(negMp[:, P:P + 1],
                                                     ps_m[:, 8 + P:9 + P]))
            NOPS[0] = len(OPS)

            def run_ops(eng, code):
                eng.wait_ge(s_bnm, 16)
                for k, (e, fn) in enumerate(OPS):
                    if e == code:
                        eng.wait_ge(s_bnc, k)
                        fn(eng).then_inc(s_bnc, 1)

            @blk.vector
            def _(v):
                run_ops(v, "v")

            @blk.scalar
            def _(a):
                run_ops(a, "a")

            @blk.tensor
            def _(te):
                run_ops(te, "t")
        reg_pj.close()
        reg_bN.close()
        reg_bA.close()

        reg_at = ExitStack()
        ps2 = [reg_at.enter_context(nc.psum_tensor(f"ps_b2{i}", [128, 1024], F32))
               for i in range(3)]
        ctxps = [reg_at.enter_context(nc.psum_tensor(f"ps_bc{i}", [128, 512], F32))
                 for i in range(2)]
        ps_tr = ps2[0][:, 0:128].bitcast(BF16)

        def pv_of(te, uu):
            g0, P0, j0 = units[uu]
            te.wait_ge(s_bex, uu + 1)
            if j0 >= 4 * g0:
                te.wait_ge(s_bmk, mask_thru[uu])
            pi = 2 * g0 + P0
            if j0 == 4 * g0 + 3 and pi >= 1:
                te.wait_ge(s_bcx, 8 * pi)
            last = None
            for qt in range(4):
                Q = 4 * g0 + qt
                if j0 > Q:
                    continue
                base = (qt % 2) * 130
                for h01 in range(2):
                    last = te.matmul(
                        ctxps[qt // 2][:, base + h01 * 65:base + h01 * 65 + 65],
                        E2[uu % NE][:, h01 * 512 + qt * 128:
                                   h01 * 512 + (qt + 1) * 128],
                        vtokv[:, j0, (2 * P0 + h01) * 65:(2 * P0 + h01) * 65 + 65],
                        start=(j0 == Q and qt % 2 == 1 and h01 == 0),
                        stop=(j0 == 0))
            last.then_inc(s_bpv, 1)

        with nc.Block("battn") if BS >= 4 else _NullBlock() as blk:
            @blk.gpsimd
            def _(g_):
                for g in range(16):
                    if g >= 2:
                        g_.wait_ge(s_qbl[g % 2], 32 * (g // 2))
                        g_.wait_ge(s_ttc, 8 * (g - 1))
                    g_.wait_ge(s_qso, 32 * (4 * (g % 4) + g // 4 + 1))
                    for jj in range(2):
                        g_.dma_start(
                            out=qbv[g % 2][:, jj, :],
                            in_=q_scr[128 * jj:128 * (jj + 1),
                                      512 * g:512 * (g + 1)]
                        ).then_inc(s_qbl[g % 2], 16)
                    d, h4 = g // 4, g % 4
                    g_.wait_ge(s_a2o, 32 * g)
                    g_.wait_ge(s_ttc, 8 * (g + 1))
                    for fh in range(2):
                        g_.dma_start(
                            out=ag2_in[256 * d + 128 * fh:
                                       256 * d + 128 * (fh + 1),
                                       512 * h4:512 * (h4 + 1)],
                            in_=ctxTv[:, fh, :]).then_inc(s_a2o, 16)
                    if h4 == 3:
                        g_.wait_ge(s_a2o, 128 * (d + 1))
                        g_.collective_compute(
                            "AllGather", ALU.bypass, replica_groups=GROUPS,
                            ins=[ag2_in[256 * d:256 * (d + 1), :]],
                            outs=[ag2_out[1024 * d:1024 * (d + 1), :]],
                        ).then_inc(s_a2c[d], 1)

            @blk.tensor
            def _(te):
                te.wait_ge(s_bnc, NOPS[0])
                uu = 0
                qtg = 0
                for g in range(16):
                    te.wait_ge(s_qbl[g % 2], 32 * (g // 2 + 1))
                    te.wait_ge(s_ttc, 8 * g)
                    for P in range(2):
                        pend = None
                        for j in range(4 * g + 3, -1, -1):
                            if uu >= 3:
                                te.wait_ge(s_bex, uu - 2)
                            te.matmul(ps2[uu % 3][:, 0:512],
                                      kTv[0:64, P, j * 128:(j + 1) * 128],
                                      qbv[g % 2][0:64, P, :],
                                      start=True, stop=True,
                                      tile_position=(0, 0))
                            te.matmul(ps2[uu % 3][:, 512:1024],
                                      kTv[64:128, P, j * 128:(j + 1) * 128],
                                      qbv[g % 2][64:128, P, :],
                                      start=True, stop=True,
                                      tile_position=(64, 0)).then_inc(s_bs2, 1)
                            if pend is not None:
                                pv_of(te, pend)
                            pend = uu
                            uu += 1
                        pv_of(te, pend)
                    te.wait_ge(s_bcx, 16 * g + 16)
                    for qt in range(4):
                        te.wait_ge(s_ttc, 2 * qtg)
                        te.transpose(ps_tr[:, 0:128],
                                     ctxbv[:, qt, 0:128], ident[:, :])
                        te.transpose(ps_tr[:, 128:256],
                                     ctxbv[:, qt, 128:256], ident[:, :]
                                     ).then_inc(s_ttc, 1)
                        qtg += 1

            @blk.scalar
            def _(a):
                a.wait_ge(s_bnc, NOPS[0])
                for uu, (g, P, j) in enumerate(units):
                    a.wait_ge(s_bs2, uu + 1)
                    if uu >= NE:
                        a.wait_ge(s_bpv, uu - NE + 1)
                    a.activation(E2[uu % NE][:, :], ps2[uu % 3][:, :], AF.Exp,
                                 bias=negMp[:, P:P + 1]).then_inc(s_bex, 1)

            @blk.vector
            def _(v):
                dc = [0]
                for g in range(16):
                    for P in range(2):
                        for j in range(4 * g + 3, 4 * g - 1, -1):
                            uu = uof[(g, P, j)]
                            i = j - 4 * g
                            v.wait_ge(s_bex, uu + 1)
                            v.tensor_mul(E2[uu % NE][:, i * 128:(i + 1) * 128],
                                         E2[uu % NE][:, i * 128:(i + 1) * 128],
                                         tri01[:, :]).then_inc(s_bmk, 1)
                            v.tensor_mul(E2[uu % NE][:, 512 + i * 128:
                                                    512 + (i + 1) * 128],
                                         E2[uu % NE][:, 512 + i * 128:
                                                    512 + (i + 1) * 128],
                                         tri01[:, :]).then_inc(s_bmk, 1)
                        v.wait_ge(s_bpv, pair_last[(g, P)] + 1)
                        for half in range(2):
                            ins = v.tensor_single_scalar(
                                dtmp[:, 4 * half:4 * half + 4],
                                ctxps[half][:, 64:324].rearrange(
                                    "p (k s) -> p k s", s=65)[:, :, 0:1
                                ].squeeze(2),
                                1e-30, ALU.add)
                        dc[0] += 1
                        ins.then_inc(s_dnr, 1)
                        v.wait_ge(s_dnr, 2 * dc[0] - 1)
                        v.reciprocal(dtmp[:, :], dtmp[:, :]).then_inc(s_dnr, 1)
                        v.wait_ge(s_dnr, 2 * dc[0])
                        for qt in range(4):
                            base = (qt % 2) * 130
                            for h01 in range(2):
                                v.tensor_scalar_mul(
                                    ctxbv[:, qt,
                                          (2 * P + h01) * 64:(2 * P + h01 + 1) * 64],
                                    ctxps[qt // 2][:, base + h01 * 65:
                                                   base + h01 * 65 + 64],
                                    dtmp[:, 2 * qt + h01:2 * qt + h01 + 1]
                                ).then_inc(s_bcx, 1)
                    for qt in range(4):
                        Qg = 4 * g + qt
                        qb, l = Qg // 4, Qg % 4
                        v.wait_ge(s_ttc, 2 * Qg + 1)
                        if qb >= 1 and l == 0:
                            v.wait_ge(s_a2o, 32 * qb)
                        v.tensor_copy(
                            ctxTv[:, :, l * 128:(l + 1) * 128],
                            ps_tr[:, :].rearrange("p (fh c) -> p fh c", c=128)
                        ).then_inc(s_ttc, 1)

        reg_at.close()
        reg_bB.close()
        reg_bC = ExitStack()
        ctxg = sbP("ctxg", [128, 8 * 2048], BF16, reg_bC)
        flshb = sqdump[:, 0:512]
        ctxgv = ctxg[:, :].rearrange("p (i t) -> p i t", i=8)
        ps_op = [reg_bC.enter_context(nc.psum_tensor(f"ps_bo{i}", [128, 512], F32))
                 for i in range(2)]

        with nc.Block("bout") if BS >= 5 else _NullBlock() as blk:
            @blk.gpsimd
            def _(g_):
                g_.wait_ge(sem("b_wb"), 64)
                for d in range(4):
                    g_.wait_ge(s_a2c[d], 1)
                for ft in range(8):
                    if ft >= 1:
                        g_.wait_ge(s_cgl, 16 * ft)
                    g_.indirect_dma_start(
                        out=ctxgv[:, ft, :], out_offset=None,
                        in_=ag2_out[0:128, :],
                        in_offset=bass.IndirectOffsetOnAxis(gixs[:, ft:ft + 1], 0),
                    ).then_inc(s_cgl, 16)
                g_.wait_ge(s_cgl, 128)
                g_.dma_start(out=flshb[:, :], in_=ag2_out[0:128, 0:512]
                             ).then_inc(s_cgl, 16)

            @blk.tensor
            def _(te):
                te.wait_ge(s_cgl, 144)
                te.wait_ge(s_bwc, 59)
                te.wait_ge(s_bex, NU_B)
                te.wait_ge(s_bcx, 256)
                te.wait_ge(s_ttc, 128)
                for seq in range(32):
                    u2 = seq
                    m, o = u2 // 2, u2 % 2
                    if seq >= 2:
                        te.wait_ge(s_box, seq - 1)
                    for ft in range(8):
                        te.matmul(ps_op[seq % 2][:, :],
                                  ctxgv[:, ft, m * 128:(m + 1) * 128],
                                  wpvv[:, ft, o * 512:(o + 1) * 512],
                                  start=(ft == 0), stop=False)
                    te.matmul(ps_op[seq % 2][:, :], onesbf[0:1, 0:128],
                              bprbf[0:1, o * 512:(o + 1) * 512],
                              start=False, stop=True).then_inc(s_bom, 1)

            @blk.vector
            def _(v):
                for seq in range(32):
                    u2 = seq
                    m, o = u2 // 2, u2 % 2
                    v.wait_ge(s_bom, seq + 1)
                    v.tensor_add(xv[:, m, o * 512:(o + 1) * 512],
                                 xv[:, m, o * 512:(o + 1) * 512],
                                 ps_op[seq % 2][:, :]).then_inc(s_box, 1)
        dbg["vtok"] = (vtok[:, 0:16384], 16384)
        dbg["negm"] = (negMp[:, 0:2].bitcast(BF16), 4)
        dbg["kt"] = (xnT[:, 0:16384], 16384)
        dbg["qscr"] = (q_scr[:, :].rearrange("(jj p) t -> p jj t", p=128), -8192)
        dbg["ctxg"] = (ctxg[:, 0:16384], 16384)
        reg_bC.close()
        reg_bP.close()
        nw["reg"].close()
        if BS >= 5:
            return (s_box, 32)
        if BS == 4:
            return (s_ttc, 128)
        if BS == 3:
            return (s_bnc, NOPS[0])
        if BS == 2:
            return (s_ev, 128)
        return ln3_done

    if PH >= 4:
        fxv = final_wait[0]
        fxv_base = final_wait[1] - 32
        ln3_done = ffn_ln3[0]
        battn_done = emit_attn_b(ln3_done, gate_w=(fxv, fxv_base + 32))
        final_wait = battn_done

    if PH >= 5:
        prep2 = emit_ffn_prep("fprep2", (sems["b_om"], 32))
        box = final_wait[0]
        ln4_done = emit_ln("ln4", [],
                           tile_waits=lambda t: [(box, 2 * t + 2)],
                           keep_reg=True)
        ffn4_done = emit_ffn("ffn4", ln4_done[:2], prep2,
                             extra_te_waits=[(box, 32)])
        final_wait = ffn4_done
        ln4_done[2].close()
        prep2[3].close()

    if _DUMP is not None:
        dbg_d = nc.dram_tensor("dbg", [128, 16384], BF16, kind="ExternalOutput")
        dbg["xnt"] = (xnT[:, 0:16384], 16384)
        src, ncols = dbg[_DUMP]
        if ncols < 0:
            out_ap = dbg_d[:, 0:16384].rearrange("p (a b) -> p a b", b=-ncols)
        else:
            out_ap = dbg_d[:, 0:ncols]
        sdmp = sem("dmp")
        with nc.Block("dump") as blk:
            @blk.gpsimd
            def _(g):
                g.wait_ge(*final_wait)
                g.dma_start(out=out_ap, in_=src
                            ).then_inc(sdmp, 16)
                g.wait_ge(sdmp, 16)

    stv = sem("stv")
    with nc.Block("store") as blk:
        @blk.sync
        def _(sync):
            sync.wait_ge(*final_wait)
            sync.dma_start(out=out_d.rearrange("(t p) f -> p t f", p=128),
                           in_=xv).then_inc(stv, 16)
            sync.wait_ge(stv, 16)

    ctx.close()
    return nc


# version 28 (final)
# speedup vs baseline: 1.0319x; 1.0319x over previous
import numpy as np
import ml_dtypes
from contextlib import ExitStack
import concourse.bass as bass
import concourse.mybir as mybir
from concourse.bass_utils import run_bass_kernel_spmd

F32 = mybir.dt.float32
BF16 = mybir.dt.bfloat16
AF = mybir.ActivationFunctionType
ALU = mybir.AluOpType
AXX = mybir.AxisListType.X

B, T, D = 2, 8192, 1024
OWN = 2048
TT = OWN // 128
GROUPS = [[0, 1, 2, 3], [4, 5, 6, 7]]
LN_EPS = 1e-5
A_EPS = 1e-6

_BUILD_PHASES = 99
_A_STAGE = 99
_F_STAGE = 99
_B_STAGE = 99
_DUMP = None
_DUMP_BUF = None



def _stage(inputs):
    x_full = np.asarray(inputs["L0_attn_a_x"], np.float32)
    wqkv = np.asarray(inputs["L0_attn_b_w_qkv"], np.float32)
    bqkv = np.asarray(inputs["L0_attn_b_b_qkv"], np.float32)
    per_core = []
    for c in range(8):
        b, p = c // 4, c % 4
        heads0 = 4 * p
        cols = np.concatenate([np.arange(part * D + heads0 * 64,
                                         part * D + heads0 * 64 + 256)
                               for part in range(3)])
        expn = np.zeros((1, 4), np.float32)
        flg = np.zeros((1, 4), np.float32)
        for r in range(4):
            if r < p:
                expn[0, r] = 2048.0 * (p - 1 - r)
                flg[0, r] = 1.0
        gix = np.zeros((128, 8), np.int32)
        for ft in range(8):
            gix[:, ft] = 1024 * p + 128 * ft + np.arange(128)
        d = {
            "x": np.ascontiguousarray(x_full[b, 2048 * p:2048 * (p + 1)]),
            "bqvr": np.ascontiguousarray(bqkv[cols[512:768]].reshape(1, 256)),
            "gix": gix,
            "wq": np.asarray(inputs["L0_attn_a_W_Q"], np.float32),
            "wk": np.asarray(inputs["L0_attn_a_W_K"], np.float32),
            "wv": np.asarray(inputs["L0_attn_a_W_V"], np.float32),
            "wg": np.asarray(inputs["L0_attn_a_W_g_w"], np.float32),
            "wo": np.asarray(inputs["L0_attn_a_W_O"], np.float32),
            "dl": np.asarray(inputs["L0_attn_a_decay_log"], np.float32).reshape(1, 8),
            "wqkv": np.ascontiguousarray(wqkv[:, cols]),
            "bqkv": np.ascontiguousarray(bqkv[cols].reshape(6, 128).T),
            "wproj": np.asarray(inputs["L0_attn_b_w_proj"], np.float32),
            "theta": np.asarray(inputs["L0_ffn_theta"], np.float32),
            "phi": np.asarray(inputs["L0_ffn_phi"], np.float32),
            "thn": np.asarray(inputs["L0_ffn_theta_noise"], np.float32),
            "phn": np.asarray(inputs["L0_ffn_phi_noise"], np.float32),
            "wgbr": np.asarray(inputs["L0_attn_a_W_g_b"], np.float32).reshape(1, 512),
            "bpr": np.asarray(inputs["L0_attn_b_b_proj"], np.float32).reshape(1, 1024),
            "expn": expn,
            "flg": flg,
        }
        per_core.append(d)
    return per_core


def kernel(**inputs):
    shards = _stage(inputs)
    nc = _build()
    res = run_bass_kernel_spmd(nc, shards, core_ids=list(range(8)))
    out = np.zeros((B, T, D), np.float32)
    for c in range(8):
        b, p = c // 4, c % 4
        out[b, 2048 * p:2048 * (p + 1)] = np.asarray(res.results[c]["out"], np.float32)
    kernel._last = res
    return out



class _NullBlock:
    def __enter__(self):
        return self

    def __exit__(self, *a):
        return False

    def _noop(self, f):
        return None
    sync = scalar = vector = tensor = gpsimd = _noop


def _build():
    nc = bass.Bass("TRN2", target_bir_lowering=False,
                   dynamic_dma_scratch_size=16384)
    PH = _BUILD_PHASES

    x_d = nc.dram_tensor("x", [OWN, D], F32, kind="ExternalInput")
    wq_d = nc.dram_tensor("wq", [512, 1024], F32, kind="ExternalInput")
    wk_d = nc.dram_tensor("wk", [512, 1024], F32, kind="ExternalInput")
    wv_d = nc.dram_tensor("wv", [512, 1024], F32, kind="ExternalInput")
    wg_d = nc.dram_tensor("wg", [512, 1024], F32, kind="ExternalInput")
    wo_d = nc.dram_tensor("wo", [1024, 512], F32, kind="ExternalInput")
    dl_d = nc.dram_tensor("dl", [1, 8], F32, kind="ExternalInput")
    wqkv_d = nc.dram_tensor("wqkv", [1024, 768], F32, kind="ExternalInput")
    bqkv_d = nc.dram_tensor("bqkv", [128, 6], F32, kind="ExternalInput")
    wproj_d = nc.dram_tensor("wproj", [1024, 1024], F32, kind="ExternalInput")
    th_d = nc.dram_tensor("theta", [1024, 1024], F32, kind="ExternalInput")
    ph_d = nc.dram_tensor("phi", [1024, 1024], F32, kind="ExternalInput")
    thn_d = nc.dram_tensor("thn", [1024, 1024], F32, kind="ExternalInput")
    phn_d = nc.dram_tensor("phn", [1024, 1024], F32, kind="ExternalInput")
    wgbr_d = nc.dram_tensor("wgbr", [1, 512], F32, kind="ExternalInput")
    bpr_d = nc.dram_tensor("bpr", [1, 1024], F32, kind="ExternalInput")
    expn_d = nc.dram_tensor("expn", [1, 4], F32, kind="ExternalInput")
    flg_d = nc.dram_tensor("flg", [1, 4], F32, kind="ExternalInput")
    out_d = nc.dram_tensor("out", [OWN, D], F32, kind="ExternalOutput")

    bqvr_d = nc.dram_tensor("bqvr", [1, 256], F32, kind="ExternalInput")
    gix_d = nc.dram_tensor("gix", [128, 8], mybir.dt.int32, kind="ExternalInput")

    aga_in = nc.dram_tensor("aga_in", [64, 520], F32)
    aga_out = nc.dram_tensor("aga_out", [256, 520], F32)
    ag1_in = nc.dram_tensor("ag1_in", [4096, 512], BF16)
    ag1_out = nc.dram_tensor("ag1_out", [16384, 512], BF16)
    q_scr = nc.dram_tensor("q_scr", [256, 8192], BF16)
    ag2_in = nc.dram_tensor("ag2_in", [1024, OWN], BF16)
    ag2_out = nc.dram_tensor("ag2_out", [4096, OWN], BF16)

    dbg = {}
    ctx = ExitStack()
    E = ctx.enter_context

    def sb(name, shape, dtype):
        return E(nc.sbuf_tensor(name, shape, dtype))

    sems = {}

    def sem(name):
        if name not in sems:
            sems[name] = E(nc.semaphore(name))
        return sems[name]

    x_sb = sb("x_sb", [128, TT * 1024], F32)
    xv = x_sb[:, :].rearrange("p (t f) -> p t f", f=1024)
    xnT = sb("xnT", [128, 8 * OWN], BF16)
    xnTv = xnT[:, :].rearrange("p (ft t) -> p ft t", t=OWN)
    xns2 = sb("xns2", [128, 2 * 1024], BF16)

    ident = sb("ident", [128, 128], BF16)
    tri01 = sb("tri01", [128, 128], BF16)
    trif = sb("trif", [128, 128], BF16)
    emat = sb("emat", [128, 128], BF16)
    iotac = sb("iotac", [128, 1], F32)
    ones1f = sb("ones1f", [1, 128], F32)
    onesc = sb("onesc", [128, 1], F32)
    onesden = sb("onesden", [128, 64], BF16)
    scr1 = sb("scr1", [128, 128], BF16)
    c_lneps = sb("c_lneps", [128, 1], F32)
    c_pihalf = sb("c_pihalf", [128, 1], F32)

    ln_sums = sb("ln_sums", [128, TT], F32)
    ln_mneg = sb("ln_mneg", [128, TT], F32)
    ln_var = sb("ln_var", [128, TT], F32)
    ln_rstd = sb("ln_rstd", [128, TT], F32)
    sqdump = sb("sqdump", [128, 1024], BF16)

    ldw = sem("ldw")
    _ldw_n = [0]

    def dma(eng, out, in_):
        eng.dma_start(out=out, in_=in_).then_inc(ldw, 16)
        _ldw_n[0] += 16
        return _ldw_n[0]

    x_ready = [0]
    with nc.Block("ph0") as blk:
        @blk.sync
        def _(sync):
            x_ready[0] = dma(sync, xv, x_d.rearrange("(t p) f -> p t f", p=128))

        @blk.gpsimd
        def _(g):
            s0 = sem("c0chain")
            k = [0]

            def step(ins):
                ins.then_inc(s0, 1)
                k[0] += 1
                g.wait_ge(s0, k[0])
            step(g.memset(scr1[:, :], 1.0))
            step(g.affine_select(out=scr1[:, :], in_=scr1[:, :],
                                 compare_op=ALU.is_ge, fill=0.0, base=0,
                                 pattern=[[1, 128]], channel_multiplier=-1))
            step(g.tensor_copy(tri01[:, :], scr1[:, :]))
            step(g.tensor_copy(trif[:, :], scr1[:, :]))
            step(g.affine_select(out=scr1[:, :], in_=scr1[:, :],
                                 compare_op=ALU.is_ge, fill=0.0, base=0,
                                 pattern=[[-1, 128]], channel_multiplier=1))
            step(g.tensor_copy(ident[:, :], scr1[:, :]))
            g.iota(emat[:, :], [[1, 128]], base=0, channel_multiplier=-1,
                   allow_small_or_imprecise_dtypes=True)
            g.iota(iotac[:, :], [[1, 1]], base=0, channel_multiplier=1,
                   allow_small_or_imprecise_dtypes=True)
            g.memset(ones1f[:, :], 1.0)
            g.memset(onesc[:, :], 1.0)
            g.memset(onesden[:, :], 1.0)
            g.memset(c_lneps[:, :], LN_EPS)
            g.memset(c_pihalf[:, :], float(np.pi / 2)).then_inc(sem("tab0"), 1)

    ln_count = [0]

    def emit_ln(tag, waits, tile_waits=None, keep_reg=False):
        i = ln_count[0]
        ln_count[0] += 1
        s_v1 = sem("ln_v1")
        s_a1 = sem("ln_a1")
        s_nm, s_tp, s_cp = sem("ln_nm"), sem("ln_tp"), sem("ln_cp")
        bV, bA, bT = i * 2 * TT, i * 3 * TT, i * TT
        xns2v = xns2[:, :].rearrange("p (b f) -> p b f", f=1024)
        reg_ln = ExitStack()
        ps_ln = [reg_ln.enter_context(
            nc.psum_tensor(f"ps_ln{i}_{k}", [128, 1024], BF16)) for k in range(2)]

        with nc.Block(f"ln_{tag}") as blk:
            @blk.vector
            def _(v):
                for s, n in waits:
                    v.wait_ge(s, n)

                def stats(t):
                    if tile_waits is not None:
                        for s, n in tile_waits(t):
                            v.wait_ge(s, n)
                    v.reduce_sum(ln_sums[:, t:t + 1], xv[:, t, :], axis=AXX
                                 ).then_inc(s_v1, 1)
                    v.wait_ge(s_v1, bV + 2 * t + 1)
                    v.tensor_scalar_mul(ln_mneg[:, t:t + 1],
                                        ln_sums[:, t:t + 1], -1.0 / 1024
                                        ).then_inc(s_v1, 1)

                def norm(t):
                    v.wait_ge(s_a1, bA + 3 * t + 3)
                    if t >= 2:
                        v.wait_ge(s_tp, bT + t - 1)
                    v.tensor_scalar(xns2v[:, t % 2, :], xv[:, t, :],
                                    ln_mneg[:, t:t + 1], ln_rstd[:, t:t + 1],
                                    ALU.add, ALU.mult).then_inc(s_nm, 1)

                def copy(t):
                    v.wait_ge(s_tp, bT + t + 1)
                    v.tensor_copy(xnTv[:, :, t * 128:(t + 1) * 128],
                                  ps_ln[t % 2][:, :].rearrange(
                                      "p (ft c) -> p ft c", c=128)
                                  ).then_inc(s_cp, 1)
                for t in range(TT):
                    stats(t)
                    if t >= 1:
                        norm(t - 1)
                    if t >= 2:
                        copy(t - 2)
                norm(TT - 1)
                copy(TT - 2)
                copy(TT - 1)

            @blk.scalar
            def _(a):
                for s, n in waits:
                    a.wait_ge(s, n)
                a.wait_ge(sem("tab0"), 1)
                for t in range(TT):
                    if tile_waits is not None:
                        for s, n in tile_waits(t):
                            a.wait_ge(s, n)
                    a.wait_ge(s_v1, bV + 2 * t + 2)
                    a.activation(sqdump[:, :], xv[:, t, :], AF.Square,
                                 bias=ln_mneg[:, t:t + 1], scale=1.0,
                                 accum_out=ln_var[:, t:t + 1]).then_inc(s_a1, 1)
                    a.wait_ge(s_a1, bA + 3 * t + 1)
                    a.activation(ln_var[:, t:t + 1], ln_var[:, t:t + 1],
                                 AF.Ln, bias=c_lneps[:, 0:1], scale=1.0 / 1024
                                 ).then_inc(s_a1, 1)
                    a.wait_ge(s_a1, bA + 3 * t + 2)
                    a.activation(ln_rstd[:, t:t + 1], ln_var[:, t:t + 1],
                                 AF.Exp, scale=-0.5).then_inc(s_a1, 1)

            @blk.tensor
            def _(te):
                te.wait_ge(sem("tab0"), 1)
                for t in range(TT):
                    te.wait_ge(s_nm, bT + t + 1)
                    if t >= 2:
                        te.wait_ge(s_cp, bT + t - 1)
                    for ft in range(8):
                        ins = te.transpose(
                            ps_ln[t % 2][:, ft * 128:(ft + 1) * 128],
                            xns2v[:, t % 2, ft * 128:(ft + 1) * 128], ident[:, :])
                    ins.then_inc(s_tp, 1)
        if keep_reg:
            return s_cp, bT + TT, reg_ln
        reg_ln.close()
        return s_cp, bT + TT

    if PH >= 1:
        ln1_done = emit_ln("ln1", [(ldw, x_ready[0])])
    final_wait = ln1_done if PH >= 1 else (ldw, x_ready[0])

    AS = _A_STAGE
    if PH >= 2:
        reg_a = ExitStack()

        def sba(name, shape, dtype, reg=None):
            return (reg or reg_a).enter_context(nc.sbuf_tensor(name, shape, dtype))

        wgT = sba("wgT", [128, 8 * 512], BF16)
        woT = sba("woT", [128, 4 * 1024], BF16)
        qT = sba("qT", [128, 4 * 2048], BF16)
        wraw = qT
        qTv = qT[:, :].rearrange("p (o t) -> p o t", t=2048)
        num_sb = sba("num_sb", [128, TT * 520], BF16)
        numv = num_sb[:, :].rearrange("p (m f) -> p m f", f=520)
        hist = sba("hist", [128, (TT // 2) * 520], BF16)

        def histv(m):
            return hist[(m % 2) * 64:(m % 2) * 64 + 64,
                        (m // 2) * 520:(m // 2) * 520 + 520]

        Pst = sba("Pst", [64, 520], F32)
        ptmp = sba("ptmp", [64, 520], F32)
        lng_bc = sba("lng_bc", [128, 8], F32)
        decvec = sba("decvec", [128, 8], F32)
        gam1p = sba("gam1p", [128, 8], F32)
        g128mp = sba("g128mp", [64, 128], F32)
        coefc = sba("coefc", [64, 32], F32)
        dls = sba("dls", [1, 8], F32)
        dl_e = sba("dl_e", [1, 8], F32)
        dl_g = sba("dl_g", [1, 8], F32)
        dl_l = sba("dl_l", [1, 8], F32)
        coefs = sba("coefs", [1, 32], F32)
        expn_s = sba("expn_s", [1, 4], F32)
        flg_s = sba("flg_s", [1, 4], F32)
        iop1 = sba("iop1", [128, 1], F32)
        rev1 = sba("rev1", [128, 1], F32)
        bgbc = sba("bgbc", [128, 512], BF16)
        wgbr_s = sba("wgbr_s", [1, 512], F32)

        _xf = xns2[:, :].bitcast(F32)
        scrD = [_xf[:, 0:128], _xf[:, 128:256]]
        scrE = [_xf[:, 256:384], _xf[:, 384:512]]
        m0buf = [_xf[:, 0:512], _xf[:, 512:1024]]
        t3buf = [sqdump[:, 0:512], sqdump[:, 512:1024]]

        reg_s1 = ExitStack()
        kTh = sba("kTh", [128, 4 * 1024], BF16, reg_s1)
        vTh = sba("vTh", [128, 4 * 1024], BF16, reg_s1)
        kThv = kTh[:, :].rearrange("p (o t) -> p o t", t=1024)
        vThv = vTh[:, :].rearrange("p (o t) -> p o t", t=1024)
        dTf = sba("dTf", [128, 8 * 128], BF16, reg_s1)
        vaug = sba("vaug", [128, 520], BF16, reg_s1)
        vaugv = vaug[:, :].rearrange("p (h f) -> p h f", f=65)
        decrow = sba("decrow", [128, 512], BF16, reg_s1)
        growb = sba("growb", [64, 520], BF16, reg_s1)
        ktil = sba("ktil", [128, 512], BF16, reg_s1)
        abuf = [sba(f"abuf{i}", [128, 256], BF16, reg_s1) for i in range(2)]
        rowbuf = ktil[:, :].bitcast(F32)[0:1, 0:128]

        s_wtp = sem("a_wtp")
        s_wcp = sem("a_wcp")
        _wcnt = {"dma": 0, "grp": 0}

        def emit_aw(tag, specs, extra_waits=()):
            serialized = all(specs[i][4] is specs[i - 1][4]
                             for i in range(1, len(specs)))
            with nc.Block(tag) as blk:
                reg_awp = ExitStack()
                ps_w = [reg_awp.enter_context(
                    nc.psum_tensor(f"ps_{tag}{i}", [128, 1024], BF16))
                    for i in range(2)]
                g0 = _wcnt["grp"]
                s_wl = sem(f"awl_{tag}")

                @blk.gpsimd
                def _(g):
                    cum = g0
                    for w, (wd, wt, not_, nit, stg) in enumerate(specs):
                        if w >= 1 and stg is specs[w - 1][4]:
                            g.wait_ge(s_wtp, cum)
                        cum += not_
                        g.dma_start(
                            out=stg[:, 0:not_ * nit * 128].rearrange(
                                "p (o f) -> p o f", f=nit * 128),
                            in_=wd.rearrange("(o p) f -> p o f", p=128)
                        ).then_inc(s_wl, 16)

                @blk.tensor
                def _(te):
                    te.wait_ge(sem("tab0"), 1)
                    for s, n in extra_waits:
                        te.wait_ge(s, n)
                    gidx = g0
                    if not serialized:
                        te.wait_ge(s_wl, 16 * len(specs))
                    for w, (wd, wt, not_, nit, stg) in enumerate(specs):
                        if serialized:
                            te.wait_ge(s_wl, 16 * (w + 1))
                        wrv = stg[:, 0:not_ * nit * 128].rearrange(
                            "p (o f) -> p o f", f=nit * 128)
                        for ot in range(not_):
                            if gidx >= g0 + 2:
                                te.wait_ge(s_wcp, gidx - 1)
                            for it in range(nit):
                                ins = te.transpose(
                                    ps_w[gidx % 2][:, it * 128:(it + 1) * 128],
                                    wrv[:, ot, it * 128:(it + 1) * 128],
                                    ident[:, :])
                            ins.then_inc(s_wtp, 1)
                            gidx += 1

                @blk.vector
                def _(v):
                    gidx = g0
                    for w, (wd, wt, not_, nit, stg) in enumerate(specs):
                        wtv = wt[:, :].rearrange("p (i f) -> p i f", i=nit)
                        for ot in range(not_):
                            v.wait_ge(s_wtp, gidx + 1)
                            v.tensor_copy(
                                wtv[:, :, ot * 128:(ot + 1) * 128],
                                ps_w[gidx % 2][:, 0:nit * 128].rearrange(
                                    "p (i c) -> p i c", c=128)).then_inc(s_wcp, 1)
                            gidx += 1
                _wcnt["grp"] = g0 + sum(s[2] for s in specs)
                reg_awp.close()
            return _wcnt["grp"]

        reg_wq = ExitStack()
        wqT = sba("wqT", [128, 8 * 512], BF16, reg_wq)
        AW1_DONE = None

        AW1_DONE = 16
        if AS >= 0:
            emit_aw("aw1", [(wg_d, wgT, 4, 8, wraw),
                            (wo_d, woT, 8, 4, wraw),
                            (wq_d, wqT, 4, 8, wraw)],
                    extra_waits=[ln1_done])
            final_wait = (sem("a_wcp"), AW1_DONE)

        with nc.Block("atab") if AS >= 1 else _NullBlock() as blk:
            reg_tbp = ExitStack()
            ps_bc = reg_tbp.enter_context(nc.psum_tensor("ps_bc", [128, 512], F32))
            s_tl = sem("at_l")
            s_tt = sem("at_tt")
            OPS = []

            def op(eng, fn):
                OPS.append((eng, fn))

            op("a", lambda a: a.activation(dl_e[:, :], dls[:, :], AF.Exp, scale=-1.0))
            op("v", lambda v: v.tensor_scalar_add(dl_g[:, :], dl_e[:, :], 1.0))
            op("v", lambda v: v.reciprocal(dl_g[:, :], dl_g[:, :]))
            op("v", lambda v: v.tensor_scalar_max(dl_g[:, :], dl_g[:, :], 1e-37))
            op("a", lambda a: a.activation(dl_l[:, :], dl_g[:, :], AF.Ln))
            op("t", lambda t: t.matmul(ps_bc[:, 0:8], ones1f[:, :], dl_l[:, :],
                                       start=True, stop=True))
            op("v", lambda v: v.tensor_copy(lng_bc[:, :], ps_bc[:, 0:8]))
            for m in range(16):
                op("a", lambda a, m=m: a.activation(
                    rowbuf[:, m * 8:(m + 1) * 8], dl_l[:, :], AF.Exp, scale=128.0 * m))
            op("t", lambda t: t.matmul(ps_bc[0:64, 128:256], ones1f[:, 0:64],
                                       rowbuf[:, :], start=True, stop=True))
            op("v", lambda v: v.tensor_copy(g128mp[:, :], ps_bc[0:64, 128:256]))
            for r in range(4):
                op("a", lambda a, r=r: a.activation(
                    coefs[:, r * 8:(r + 1) * 8], dl_l[:, :], AF.Exp,
                    scale=expn_s[0:1, r:r + 1]))
            for r in range(4):
                op("v", lambda v, r=r: v.tensor_scalar_mul(
                    coefs[:, r * 8:(r + 1) * 8], coefs[:, r * 8:(r + 1) * 8],
                    flg_s[0:1, r:r + 1]))
            op("t", lambda t: t.matmul(ps_bc[0:64, 256:288], ones1f[:, 0:64],
                                       coefs[:, :], start=True, stop=True))
            op("v", lambda v: v.tensor_copy(coefc[:, :], ps_bc[0:64, 256:288]))
            op("v", lambda v: v.tensor_scalar_add(iop1[:, :], iotac[:, :], 1.0))
            op("v", lambda v: v.tensor_scalar(rev1[:, :], iotac[:, :], -1.0, 127.0,
                                              ALU.mult, ALU.add))
            op("v", lambda v: v.tensor_scalar_mul(scrD[0][:, 0:8], lng_bc[:, :],
                                                  iop1[:, 0:1]))
            op("v", lambda v: v.tensor_scalar_mul(scrD[1][:, 0:8], lng_bc[:, :],
                                                  rev1[:, 0:1]))
            op("a", lambda a: a.activation(gam1p[:, :], scrD[0][:, 0:8], AF.Exp))
            op("a", lambda a: a.activation(decvec[:, :], scrD[1][:, 0:8], AF.Exp))
            op("v", lambda v: v.memset(decrow[:, :], 1.0))
            for h in range(8):
                op("v", lambda v, h=h: v.tensor_scalar_mul(
                    decrow[:, h * 64:(h + 1) * 64],
                    decrow[:, h * 64:(h + 1) * 64], decvec[:, h:h + 1]))
            op("v", lambda v: v.memset(growb[:, :], 1.0))
            for h in range(8):
                op("v", lambda v, h=h: v.tensor_scalar_mul(
                    growb[:, h * 65:(h + 1) * 65],
                    growb[:, h * 65:(h + 1) * 65], g128mp[:, 8 + h:9 + h]))
            for h in range(8):
                op("v", lambda v, h=h: v.tensor_scalar(
                    scrD[h % 2][:, :], emat[:, :], lng_bc[:, h:h + 1], 0.0,
                    ALU.mult, ALU.min))
                op("a", lambda a, h=h: a.activation(scrE[h % 2][:, :],
                                                    scrD[h % 2][:, :], AF.Exp))
                op("v", lambda v, h=h: v.tensor_mul(
                    dTf[:, h * 128:(h + 1) * 128], scrE[h % 2][:, :], trif[:, :]))
            op("t", lambda t: t.matmul(ps_bc[:, 0:512], ones1f[:, :], wgbr_s[:, :],
                                       start=True, stop=True))
            op("v", lambda v: v.tensor_copy(bgbc[:, :], ps_bc[:, 0:512]))
            for h in range(8):
                op("v", lambda v, h=h: v.memset(
                    vaug[:, h * 65 + 64:h * 65 + 65], 1.0))
            TABN = len(OPS)

            @blk.sync
            def _(sync):
                sync.dma_start(out=dls[:, :], in_=dl_d[:, :]).then_inc(s_tl, 16)
                sync.dma_start(out=expn_s[:, :], in_=expn_d[:, :]).then_inc(s_tl, 16)
                sync.dma_start(out=flg_s[:, :], in_=flg_d[:, :]).then_inc(s_tl, 16)
                sync.dma_start(out=wgbr_s[:, :], in_=wgbr_d[:, :]).then_inc(s_tl, 16)

            def run_ops(eng, code):
                eng.wait_ge(s_tl, 64)
                eng.wait_ge(sem("tab0"), 1)
                eng.wait_ge(ln1_done[0], ln1_done[1])
                eng.wait_ge(sem("a_wcp"), 16)
                for k, (e, fn) in enumerate(OPS):
                    if e == code:
                        eng.wait_ge(s_tt, k)
                        fn(eng).then_inc(s_tt, 1)

            @blk.scalar
            def _(a):
                run_ops(a, "a")

            @blk.vector
            def _(v):
                run_ops(v, "v")

            @blk.tensor
            def _(te):
                run_ops(te, "t")
        reg_tbp.close()



        reg_pp = ExitStack()
        ps_p = [reg_pp.enter_context(
            nc.psum_tensor(f"ps_p{i}", [128, 512], F32)) for i in range(2)]
        s_pp = sem("ap_pp")
        s_dm = sem("ap_dm")
        s_ph = sem("ap_ph")
        s_p3 = sem("ap_p3")
        s_da = sem("ap_da")
        s_vc = sem("ap_vc")
        UKIND = []

        def _bank_wait(te, u):
            if u >= 2:
                k2, i2 = UKIND[u - 2]
                if k2 == "qk":
                    te.wait_ge(s_ph, 2 * (i2 + 1))
                else:
                    te.wait_ge(s_vc, i2 + 1)

        def emit_proj(tag, units, pe_waits):
            u0 = len(UKIND)
            j0 = sum(1 for k, i in UKIND if k == "qk")
            w0 = sum(1 for k, i in UKIND if k == "v")
            jj, ww = j0, w0
            for kind, _, _, _ in units:
                if kind == "qk":
                    UKIND.append(("qk", jj)); jj += 1
                else:
                    UKIND.append(("v", ww)); ww += 1

            with nc.Block(tag) as blk:
                @blk.tensor
                def _(te):
                    for s, n in pe_waits:
                        te.wait_ge(s, n)
                    for ui, (kind, wtv, ov, rhs) in enumerate(units):
                        u = u0 + ui
                        _bank_wait(te, u)
                        for it in range(8):
                            ins = te.matmul(ps_p[u % 2][:, :],
                                            wtv[:, it, :], rhs[:, it, :],
                                            start=(it == 0), stop=(it == 7))
                        ins.then_inc(s_pp, 1)

                @blk.vector
                def _(v):
                    qk_uis = [ui for ui, (k, _, _, _) in enumerate(units)
                              if k == "qk"]
                    for ui in qk_uis:
                        kind, wtv, ov, rhs = units[ui]
                        j = UKIND[u0 + ui][1]
                        v.wait_ge(s_p3, j + 1)
                        v.wait_ge(s_ph, 2 * j + 1)
                        v.tensor_add(ov, ov, t3buf[j % 2][:, :]).then_inc(s_da, 1)

                @blk.scalar
                def _(a):
                    prev = [None]

                    def t3_of(j):
                        a.wait_ge(s_ph, 2 * (j + 1))
                        if j >= 2:
                            a.wait_ge(s_da, j - 1)
                        a.activation(t3buf[j % 2][:, :], m0buf[j % 2][:, :],
                                     AF.Exp, scale=-1.0).then_inc(s_p3, 1)
                    for ui, (kind, wtv, ov, rhs) in enumerate(units):
                        u = u0 + ui
                        a.wait_ge(s_pp, u + 1)
                        if kind == "qk":
                            j = UKIND[u][1]
                            a.activation(ov, ps_p[u % 2][:, :],
                                         AF.Relu).then_inc(s_ph, 1)
                            if j >= 2:
                                a.wait_ge(s_p3, j - 1)
                            a.activation(m0buf[j % 2][:, :], ps_p[u % 2][:, :],
                                         AF.Relu, scale=-1.0).then_inc(s_ph, 1)
                            if prev[0] is not None:
                                t3_of(prev[0])
                            prev[0] = j
                        else:
                            a.activation(ov, ps_p[u % 2][:, :],
                                         AF.Identity).then_inc(s_vc, 1)
                    if prev[0] is not None:
                        t3_of(prev[0])

        def _wtv(wT):
            return wT[:, :].rearrange("p (i f) -> p i f", i=8)

        def _rhs(tc):
            return xnTv[:, :, tc * 512:(tc + 1) * 512]

        qunits = [("qk", _wtv(wqT)[:, :, ot * 128:(ot + 1) * 128],
                   qTv[:, ot, tc * 512:(tc + 1) * 512], _rhs(tc))
                  for ot in range(4) for tc in range(4)]
        if AS >= 2:
            emit_proj("qproj", qunits,
                      [(ln1_done[0], ln1_done[1]), (s_wcp, AW1_DONE),
                       (sem("at_tt"), TABN)])
            final_wait = (s_da, 16)
        reg_wq.close()

        reg_s2 = ExitStack()
        wkT = sba("wkT", [128, 8 * 512], BF16, reg_s2)
        wvT = sba("wvT", [128, 8 * 512], BF16, reg_s2)
        AW2_DONE = 24
        if AS >= 3:
            emit_aw("aw2", [(wk_d, wkT, 4, 8, kTh),
                            (wv_d, wvT, 4, 8, vTh)],
                    extra_waits=[(s_pp, 16)])
            final_wait = (s_wcp, AW2_DONE)

        reg_sc = ExitStack()
        ps_vt = reg_sc.enter_context(nc.psum_tensor("ps_vt", [128, 1024], BF16))
        ps_n = [reg_sc.enter_context(
            nc.psum_tensor(f"ps_n{i}", [128, 260], F32)) for i in range(2)]
        ps_s = [reg_sc.enter_context(
            nc.psum_tensor(f"ps_s{i}", [64, 260], F32)) for i in range(2)]
        ps_vtv = ps_vt[:, :].rearrange("p (s c) -> p s c", c=128)
        s_tp2 = sem("as_tp")
        s_dcp = sem("as_dcp")
        s_dk = sem("as_dk")
        s_pa = sem("as_pa")
        s_aa = sem("as_aa")
        s_pi = sem("as_pi")
        s_dse = sem("as_dse")
        s_dp1 = sem("as_dp1")
        s_dst = sem("as_dst")

        def emit_kvproj(half):
            units = []
            for ot in range(4):
                for tcl in range(2):
                    units.append(("qk", _wtv(wkT)[:, :, ot * 128:(ot + 1) * 128],
                                  kThv[:, ot, tcl * 512:(tcl + 1) * 512],
                                  _rhs(half * 2 + tcl)))
            for ot in range(4):
                for tcl in range(2):
                    units.append(("v", _wtv(wvT)[:, :, ot * 128:(ot + 1) * 128],
                                  vThv[:, ot, tcl * 512:(tcl + 1) * 512],
                                  _rhs(half * 2 + tcl)))
            waits = [(s_wcp, AW2_DONE)]
            if half == 1:
                waits.append((s_dst, 16))
                waits.append((s_aa, 32))
            emit_proj(f"kvproj{half}", units, waits)

        def emit_ascan(half):
            with nc.Block(f"ascan{half}") as blk:
                @blk.tensor
                def _(te):
                    te.wait_ge(s_da, 16 + 8 * (half + 1))
                    te.wait_ge(s_vc, 8 * (half + 1))
                    for ml in range(8):
                        m = half * 8 + ml
                        if m >= 1:
                            te.wait_ge(s_dcp, m)
                            te.wait_ge(s_dk, m)
                            te.wait_ge(s_dst, 2 * m)
                        for hp in range(4):
                            ins = te.transpose(ps_vtv[:, hp, :],
                                               vThv[:, hp, ml * 128:(ml + 1) * 128],
                                               ident[:, :])
                        for hp in range(4):
                            ins = te.transpose(ps_vtv[:, 4 + hp, :],
                                               kThv[:, hp, ml * 128:(ml + 1) * 128],
                                               ident[:, :])
                        ins.then_inc(s_tp2, 1)
                        def a_pair(hp):
                            gi = 4 * m + hp
                            if gi >= 1:
                                te.wait_ge(s_aa, gi)
                            te.matmul(ps_p[0][:, 0:128],
                                      kThv[0:64, hp, ml * 128:(ml + 1) * 128],
                                      qTv[0:64, hp, m * 128:(m + 1) * 128],
                                      start=True, stop=True, tile_position=(0, 0))
                            te.matmul(ps_p[1][:, 0:128],
                                      kThv[64:128, hp, ml * 128:(ml + 1) * 128],
                                      qTv[64:128, hp, m * 128:(m + 1) * 128],
                                      start=True, stop=True,
                                      tile_position=(64, 0)).then_inc(s_pa, 1)

                        def i_pair(hp):
                            te.wait_ge(s_aa, 4 * m + hp + 1)
                            last = None
                            for hh in range(2):
                                h = 2 * hp + hh
                                last = te.matmul(
                                    ps_n[h // 4][:, (h % 4) * 65:(h % 4) * 65 + 65],
                                    abuf[hp % 2][:, hh * 128:(hh + 1) * 128],
                                    vaugv[:, h, :],
                                    start=(h % 4 == 0), stop=(h % 4 == 3))
                            for hh in range(2):
                                h = 2 * hp + hh
                                last = te.matmul(
                                    ps_s[h // 4][:, (h % 4) * 65:(h % 4) * 65 + 65],
                                    ktil[:, h * 64:(h + 1) * 64],
                                    vaugv[:, h, :],
                                    start=(h % 4 == 0), stop=(h % 4 == 3))
                            last.then_inc(s_pi, 1)
                        te.wait_ge(s_dk, m + 1)
                        te.wait_ge(s_dcp, m + 1)
                        a_pair(0)
                        a_pair(1)
                        for hp in range(2, 4):
                            i_pair(hp - 2)
                            a_pair(hp)
                        i_pair(2)
                        i_pair(3)

                @blk.vector
                def _(v):
                    if half == 0:
                        v.memset(Pst[:, :], 0.0)
                    for ml in range(8):
                        m = half * 8 + ml
                        v.wait_ge(s_tp2, m + 1)
                        v.tensor_mul(ktil[:, :], ps_vt[:, 512:1024],
                                     decrow[:, :]).then_inc(s_dk, 1)
                        v.tensor_copy(
                            vaugv[:, 0:8, 0:64],
                            ps_vt[:, 0:512].rearrange("p (h f) -> p h f", f=64)
                        ).then_inc(s_dcp, 1)
                        for hp in range(4):
                            gi = 4 * m + hp
                            v.wait_ge(s_pa, gi + 1)
                            if gi >= 2:
                                v.wait_ge(s_pi, gi - 1)
                            v.tensor_mul(abuf[hp % 2][:, 0:128],
                                         ps_p[0][:, 0:128],
                                         dTf[:, (2 * hp) * 128:(2 * hp + 1) * 128])
                            v.tensor_mul(abuf[hp % 2][:, 128:256],
                                         ps_p[1][:, 0:128],
                                         dTf[:, (2 * hp + 1) * 128:(2 * hp + 2) * 128]
                                         ).then_inc(s_aa, 1)
                        v.wait_ge(s_pi, 4 * (m + 1))
                        v.tensor_copy(numv[:, m, 0:260], ps_n[0][:, :])
                        v.tensor_copy(numv[:, m, 260:520], ps_n[1][:, :])
                        v.tensor_copy(histv(m), Pst[:, :]).then_inc(s_dse, 1)
                        v.wait_ge(s_dse, m + 1)
                        v.tensor_mul(ptmp[:, :], Pst[:, :], growb[:, :]
                                     ).then_inc(s_dp1, 1)
                        v.wait_ge(s_dp1, m + 1)
                        v.tensor_add(Pst[:, 0:260], ptmp[:, 0:260],
                                     ps_s[0][:, :]).then_inc(s_dst, 1)
                        v.tensor_add(Pst[:, 260:520], ptmp[:, 260:520],
                                     ps_s[1][:, :]).then_inc(s_dst, 1)

        if AS >= 4:
            emit_kvproj(0)
            final_wait = (s_vc, 8)
        if AS >= 5:
            emit_ascan(0)
            final_wait = (s_dst, 16)
        if AS >= 6:
            emit_kvproj(1)
            final_wait = (s_vc, 16)
        if AS >= 7:
            emit_ascan(1)
            final_wait = (s_dst, 32)
        reg_sc.close()
        reg_s2.close()
        reg_pp.close()
        reg_s1.close()

        reg_cl = ExitStack()
        sag = sba("sag", [64, 520], F32, reg_cl)
        gm2 = sba("gm2", [64, 520], F32, reg_cl)
        gmbf = [sba(f"gmbf{i}", [128, 520], BF16, reg_cl) for i in range(2)]
        ctmp = sba("ctmp", [128, 520], F32, reg_cl)
        lbuf = sba("lbuf", [128, 2 * 520], F32, reg_cl)

        def lbv(r):
            return lbuf[(r % 2) * 64:(r % 2) * 64 + 64,
                        (r // 2) * 520:(r // 2) * 520 + 520]

        reg_cr = ExitStack()
        ps_c = [reg_cr.enter_context(
            nc.psum_tensor(f"ps_c{i}", [128, 260], F32)) for i in range(2)]
        s_g = sem("ac_g")
        s_cm = sem("ac_cm")
        s_cs = sem("ac_cs")
        s_cd = sem("ac_cd")

        with (nc.Block("aag") if AS >= 8 else _NullBlock()) as blk:
            s_ao = sem("ag_o")
            s_cc = sem("ag_cc")
            s_li = sem("ag_li")
            s_cmb = sem("ag_cmb")

            @blk.sync
            def _(sync):
                sync.wait_ge(s_dst, 2 * TT)
                sync.dma_start(out=aga_in[:, :], in_=Pst[:, :]).then_inc(s_ao, 16)

            @blk.gpsimd
            def _(g):
                g.wait_ge(s_ao, 16)
                g.collective_compute(
                    "AllGather", ALU.bypass, replica_groups=GROUPS,
                    ins=[aga_in[:, :]], outs=[aga_out[:, :]]).then_inc(s_cc, 1)
                g.wait_ge(s_cc, 1)
                g.dma_start(out=lbuf[:, :].rearrange("p (r2 f) -> p r2 f", f=520),
                            in_=aga_out.rearrange("(r2 r1 p) f -> (r1 p) r2 f",
                                                  r2=2, r1=2)).then_inc(s_li, 16)

        def corr_vec(v, m, mi, phase):
            if mi >= 2:
                v.wait_ge(s_cm, mi - 1)
            if phase == 0:
                src_ap = histv(m)
            else:
                for h in range(8):
                    ins = v.tensor_scalar_mul(
                        gm2[:, h * 65:(h + 1) * 65],
                        sag[:, h * 65:(h + 1) * 65],
                        g128mp[:, m * 8 + h:m * 8 + h + 1])
                ins.then_inc(s_g, 1)
                v.wait_ge(s_g, TT + 2 * m + 1)
                src_ap = gm2[:, :]
            v.tensor_copy(gmbf[mi % 2][0:64, :], src_ap)
            v.tensor_copy(gmbf[mi % 2][64:128, :], src_ap).then_inc(s_g, 1)
            v.wait_ge(s_cm, mi + 1)
            for h in range(8):
                ins = v.tensor_scalar_mul(
                    ctmp[:, h * 65:(h + 1) * 65],
                    ps_c[h % 2][:, (h // 2) * 65:(h // 2) * 65 + 65],
                    gam1p[:, h:h + 1])
            ins.then_inc(s_cs, 1)
            v.wait_ge(s_cs, mi + 1)
            v.tensor_add(numv[:, m, :], numv[:, m, :],
                         ctmp[:, :]).then_inc(s_cd, 1)

        def corr_te(te, m, mi, phase):
            te.wait_ge(s_g, m + 1 if phase == 0 else TT + 2 * m + 2)
            if mi >= 1:
                te.wait_ge(s_cs, mi)
            last = None
            for h in range(8):
                ot, half = h // 2, h % 2
                last = te.matmul(
                    ps_c[h % 2][:, (h // 2) * 65:(h // 2) * 65 + 65],
                    qTv[half * 64:half * 64 + 64, ot,
                        m * 128:(m + 1) * 128],
                    gmbf[mi % 2][half * 64:half * 64 + 64, h * 65:h * 65 + 65],
                    start=(h < 2), stop=(h >= 6))
            last.then_inc(s_cm, 1)

        with (nc.Block("acorr0") if AS >= 9 else _NullBlock()) as blk:
            @blk.vector
            def _(v):
                v.wait_ge(s_dst, 2 * TT)
                for m in range(TT):
                    corr_vec(v, m, m, 0)

            @blk.tensor
            def _(te):
                for m in range(TT):
                    corr_te(te, m, m, 0)

        with (nc.Block("aagc") if AS >= 8 else _NullBlock()) as blk:
            @blk.vector
            def _(v):
                v.memset(sag[:, :], 0.0)
                v.wait_ge(sem("ag_li"), 16)
                for r in range(4):
                    for h in range(8):
                        ins = v.tensor_scalar_mul(
                            ptmp[:, h * 65:(h + 1) * 65],
                            lbv(r)[:, h * 65:(h + 1) * 65],
                            coefc[:, r * 8 + h:r * 8 + h + 1])
                    ins.then_inc(sem("ag_cmb"), 1)
                    v.wait_ge(sem("ag_cmb"), 2 * r + 1)
                    v.tensor_add(sag[:, :], sag[:, :],
                                 ptmp[:, :]).then_inc(sem("ag_cmb"), 1)
                    v.wait_ge(sem("ag_cmb"), 2 * r + 2)

        with (nc.Block("acorr1") if AS >= 9 else _NullBlock()) as blk:
            @blk.vector
            def _(v):
                v.wait_ge(sem("ag_cmb"), 8)
                for m in range(TT):
                    corr_vec(v, m, TT + m, 1)

            @blk.tensor
            def _(te):
                for m in range(TT):
                    corr_te(te, m, TT + m, 1)
        reg_cl.close()

        reg_fn = ExitStack()
        ps_g = [reg_fn.enter_context(
            nc.psum_tensor(f"ps_g{i}", [128, 512], F32)) for i in range(2)]
        ps_y = [reg_fn.enter_context(
            nc.psum_tensor(f"ps_y{i}", [128, 512], BF16)) for i in range(2)]
        ps_o = [reg_fn.enter_context(
            nc.psum_tensor(f"ps_o{i}", [128, 512], F32)) for i in range(2)]
        ub = [sba(f"ub{i}", [128, 512], F32, reg_fn) for i in range(2)]
        eb = [sba(f"eb{i}", [128, 512], F32, reg_fn) for i in range(2)]
        gb = [sba(f"gb{i}", [128, 512], F32, reg_fn) for i in range(2)]
        den8 = [sba(f"den8{i}", [128, 8], F32, reg_fn) for i in range(2)]
        numf = [sba(f"numf{i}", [128, 520], F32, reg_fn) for i in range(2)]
        ybuf = [sba(f"ybuf{i}", [128, 512], BF16, reg_fn) for i in range(2)]
        yfm = [sba(f"yfm{i}", [128, 512], BF16, reg_fn) for i in range(2)]
        with (nc.Block("afin") if AS >= 10 else _NullBlock()) as blk:
            s_fg = sem("af_g")
            s_fu = sem("af_u")
            s_fe = sem("af_e")
            s_fv = sem("af_v")
            s_fw = sem("af_w")
            s_fy2 = sem("af_y2")
            s_fy = sem("af_y")
            s_fc = sem("af_c")
            s_fo = sem("af_o")
            s_fx = sem("af_x")

            @blk.tensor
            def _(te):
                te.wait_ge(sem("a_wcp"), 24)
                wgtv = wgT[:, :].rearrange("p (i f) -> p i f", i=8)
                wotv = woT[:, :].rearrange("p (i f) -> p i f", i=4)
                for m in range(TT):
                    te.wait_ge(sem("ac_cd"), TT + m + 1)
                    if m >= 2:
                        te.wait_ge(s_fu, m - 1)
                    for it in range(8):
                        ins = te.matmul(ps_g[m % 2][:, :],
                                        xnTv[:, it, m * 128:(m + 1) * 128],
                                        wgtv[:, it, 0:512],
                                        start=(it == 0), stop=(it == 7))
                    ins.then_inc(s_fg, 1)
                    te.wait_ge(s_fy2, m + 1)
                    if m >= 2:
                        te.wait_ge(s_fc, m - 1)
                    for ft in range(4):
                        ins = te.transpose(ps_y[m % 2][:, ft * 128:(ft + 1) * 128],
                                           ybuf[m % 2][:, ft * 128:(ft + 1) * 128],
                                           ident[:, :])
                    ins.then_inc(s_fy, 1)
                    te.wait_ge(s_fc, m + 1)
                    if m >= 1:
                        te.wait_ge(s_fx, 2 * m)
                    for dc in range(2):
                        for ft in range(4):
                            ins = te.matmul(ps_o[dc][:, :],
                                            yfm[m % 2][:, ft * 128:(ft + 1) * 128],
                                            wotv[:, ft, dc * 512:(dc + 1) * 512],
                                            start=(ft == 0), stop=(ft == 3))
                    ins.then_inc(s_fo, 1)

            @blk.vector
            def _(v):
                for m in range(TT):
                    v.wait_ge(s_fg, m + 1)
                    if m >= 2:
                        v.wait_ge(s_fe, m - 1)
                    v.tensor_add(ub[m % 2][:, :], ps_g[m % 2][:, :],
                                 bgbc[:, :]).then_inc(s_fu, 1)
                    v.wait_ge(s_fe, m + 1)
                    if m >= 2:
                        v.wait_ge(s_fy2, m - 1)
                    v.tensor_scalar_add(gb[m % 2][:, :], eb[m % 2][:, :],
                                        1.0).then_inc(s_fv, 1)
                    v.tensor_single_scalar(
                        den8[m % 2][:, :].rearrange("p (h o) -> p h o", o=1),
                        numv[:, m, :].rearrange("p (h f) -> p h f",
                                                f=65)[:, :, 64:65],
                        A_EPS, ALU.add).then_inc(s_fv, 1)
                    v.tensor_copy(numf[m % 2][:, :], numv[:, m, :]).then_inc(s_fv, 1)
                    v.wait_ge(s_fv, 3 * (m + 1))
                    v.reciprocal(gb[m % 2][:, :], gb[m % 2][:, :]).then_inc(s_fw, 1)
                    v.reciprocal(den8[m % 2][:, :],
                                 den8[m % 2][:, :]).then_inc(s_fw, 1)
                    v.wait_ge(s_fw, 2 * (m + 1))
                    for h in range(8):
                        ins = v.scalar_tensor_tensor(
                            ybuf[m % 2][:, h * 64:(h + 1) * 64],
                            numf[m % 2][:, h * 65:h * 65 + 64],
                            den8[m % 2][:, h:h + 1],
                            gb[m % 2][:, h * 64:(h + 1) * 64],
                            ALU.mult, ALU.mult)
                    ins.then_inc(s_fy2, 1)
                    v.wait_ge(s_fy, m + 1)
                    v.tensor_copy(yfm[m % 2][:, :],
                                  ps_y[m % 2][:, :]).then_inc(s_fc, 1)
                    v.wait_ge(s_fo, m + 1)
                    v.tensor_add(xv[:, m, 0:512], xv[:, m, 0:512],
                                 ps_o[0][:, :]).then_inc(s_fx, 1)
                    v.tensor_add(xv[:, m, 512:1024], xv[:, m, 512:1024],
                                 ps_o[1][:, :]).then_inc(s_fx, 1)

            @blk.scalar
            def _(a):
                for m in range(TT):
                    a.wait_ge(s_fu, m + 1)
                    if m >= 2:
                        a.wait_ge(s_fv, 3 * (m - 2) + 1)
                    a.activation(eb[m % 2][:, :], ub[m % 2][:, :], AF.Exp,
                                 scale=-1.0).then_inc(s_fe, 1)
        reg_fn.close()
        if AS >= 9:
            reg_cr.close()
        if AS >= 8:
            final_wait = (sem("ag_cmb"), 8)
        if AS >= 9:
            final_wait = (sem("ac_cd"), 2 * TT)
        if AS >= 10:
            final_wait = (sem("af_x"), 2 * TT)
        reg_a.close()

    prep_cnt = [0]

    def emit_ffn_prep(tag, gate):
        i = prep_cnt[0]
        prep_cnt[0] += 1
        reg_w = ExitStack()
        Cv = reg_w.enter_context(nc.sbuf_tensor(f"fp{i}_C", [128, 8 * 1024], BF16))
        Sv = reg_w.enter_context(nc.sbuf_tensor(f"fp{i}_S", [128, 8 * 1024], BF16))
        Cvv = Cv[:, :].rearrange("p (i f) -> p i f", i=8)
        Svv = Sv[:, :].rearrange("p (i f) -> p i f", i=8)
        s_pl = [sem(f"fp_l{j}") for j in range(4)]
        s_pa = sem("fp_a")
        s_ps = sem("fp_s")
        bL, b16 = i * 128, i * 16
        reg_fp = ExitStack()
        stA = [reg_fp.enter_context(nc.sbuf_tensor(f"fp{i}A{j}", [128, 1024], F32))
               for j in range(4)]
        stB = [reg_fp.enter_context(nc.sbuf_tensor(f"fp{i}B{j}", [128, 1024], F32))
               for j in range(4)]
        specs = []
        for ft in range(8):
            specs.append((th_d, thn_d, ft, Cvv, c_pihalf))
        for ft in range(8):
            specs.append((ph_d, phn_d, ft, Svv, None))
        with nc.Block(tag) as blk:
            @blk.sync
            def _(sync):
                sync.wait_ge(*gate)
                for u, (bd, nd, ft, ov, bias) in enumerate(specs):
                    if u >= 4:
                        sync.wait_ge(s_ps, b16 + u - 3)
                    sync.dma_start(out=stA[u % 4][:, :],
                                   in_=bd[ft * 128:(ft + 1) * 128, :]
                                   ).then_inc(s_pl[u % 4], 16)
                    sync.dma_start(out=stB[u % 4][:, :],
                                   in_=nd[ft * 128:(ft + 1) * 128, :]
                                   ).then_inc(s_pl[u % 4], 16)

            @blk.gpsimd
            def _(g):
                for u, (bd, nd, ft, ov, bias) in enumerate(specs):
                    g.wait_ge(s_pl[u % 4], bL + 32 * (u // 4 + 1))
                    g.tensor_add(stA[u % 4][:, :], stA[u % 4][:, :],
                                 stB[u % 4][:, :]).then_inc(s_pa, 1)

            @blk.scalar
            def _(a):
                for u, (bd, nd, ft, ov, bias) in enumerate(specs):
                    a.wait_ge(s_pa, b16 + u + 1)
                    if bias is None:
                        ins = a.activation(ov[:, ft, :], stA[u % 4][:, :], AF.Sin)
                    else:
                        ins = a.activation(ov[:, ft, :], stA[u % 4][:, :], AF.Sin,
                                           bias=bias[:, 0:1])
                    ins.then_inc(s_ps, 1)
        reg_fp.close()
        return (Cvv, Svv, (s_ps, b16 + 16), reg_w)

    ffn_cnt = [0]

    def emit_ffn(tag, ln_done, prep, extra_te_waits=(), ln_fuse=False):
        Cvv, Svv, prep_done, _ = prep
        ln_base = ln_done[1] - TT
        i = ffn_cnt[0]
        ffn_cnt[0] += 1
        s_fm = sem("ff_m")
        s_fsq = sem("ff_sq")
        s_fad = sem("ff_ad")
        s_fsr = sem("ff_sr")
        s_fxv = sem("ff_xv")
        bU = i * 32
        if ln_fuse:
            iL = ln_count[0]
            ln_count[0] += 1
            sL_v1, sL_a1 = sem("ln_v1"), sem("ln_a1")
            sL_nm, sL_tp, sL_cp = sem("ln_nm"), sem("ln_tp"), sem("ln_cp")
            bVL, bAL, bTL = iL * 2 * TT, iL * 3 * TT, iL * TT
            xns2vL = xns2[:, :].rearrange("p (b f) -> p b f", f=1024)
            reg_lnF = ExitStack()
            ps_lnF = [reg_lnF.enter_context(
                nc.psum_tensor(f"ps_ln{iL}_{k}", [128, 1024], BF16))
                for k in range(2)]

            def L_stats(v, t):
                v.reduce_sum(ln_sums[:, t:t + 1], xv[:, t, :], axis=AXX
                             ).then_inc(sL_v1, 1)
                v.wait_ge(sL_v1, bVL + 2 * t + 1)
                v.tensor_scalar_mul(ln_mneg[:, t:t + 1],
                                    ln_sums[:, t:t + 1], -1.0 / 1024
                                    ).then_inc(sL_v1, 1)

            def L_norm(v, t):
                v.wait_ge(sL_a1, bAL + 3 * t + 3)
                if t >= 2:
                    v.wait_ge(sL_tp, bTL + t - 1)
                v.tensor_scalar(xns2vL[:, t % 2, :], xv[:, t, :],
                                ln_mneg[:, t:t + 1], ln_rstd[:, t:t + 1],
                                ALU.add, ALU.mult).then_inc(sL_nm, 1)

            s_agiF = [sem(f"b_agi{c}") for c in range(4)]

            def L_copy(a, t):
                a.wait_ge(sL_tp, bTL + t + 1)
                a.copy(xnTv[:, :, t * 128:(t + 1) * 128],
                       ps_lnF[t % 2][:, :].rearrange(
                           "p (ft c) -> p ft c", c=128)
                       ).then_inc(sL_cp, 1)

            def L_stage(a, c):
                a.dma_start(
                    out=ag1_in[1024 * c:1024 * (c + 1), :].rearrange(
                        "(ft p) t -> p ft t", p=128),
                    in_=xnTv[:, :, 512 * c:512 * (c + 1)]
                ).then_inc(s_agiF[c], 16)

            def L_scalar(a, t):
                a.wait_ge(sL_v1, bVL + 2 * t + 2)
                a.activation(sqdump[:, :], xv[:, t, :], AF.Square,
                             bias=ln_mneg[:, t:t + 1], scale=1.0,
                             accum_out=ln_var[:, t:t + 1]).then_inc(sL_a1, 1)
                a.wait_ge(sL_a1, bAL + 3 * t + 1)
                a.activation(ln_var[:, t:t + 1], ln_var[:, t:t + 1],
                             AF.Ln, bias=c_lneps[:, 0:1], scale=1.0 / 1024
                             ).then_inc(sL_a1, 1)
                a.wait_ge(sL_a1, bAL + 3 * t + 2)
                a.activation(ln_rstd[:, t:t + 1], ln_var[:, t:t + 1],
                             AF.Exp, scale=-0.5).then_inc(sL_a1, 1)

            def L_transpose(te, t):
                te.wait_ge(sL_nm, bTL + t + 1)
                if t >= 2:
                    te.wait_ge(sL_cp, bTL + t - 1)
                for ft in range(8):
                    ins = te.transpose(
                        ps_lnF[t % 2][:, ft * 128:(ft + 1) * 128],
                        xns2vL[:, t % 2, ft * 128:(ft + 1) * 128], ident[:, :])
                ins.then_inc(sL_tp, 1)

        reg_ff = ExitStack()
        ps_r = [reg_ff.enter_context(
            nc.psum_tensor(f"ff{i}r{j}", [128, 512], F32)) for j in range(2)]
        ps_i = [reg_ff.enter_context(
            nc.psum_tensor(f"ff{i}i{j}", [128, 512], F32)) for j in range(2)]
        sq_r = [reg_ff.enter_context(
            nc.sbuf_tensor(f"ff{i}qr{j}", [128, 512], F32)) for j in range(2)]
        sq_i = [reg_ff.enter_context(
            nc.sbuf_tensor(f"ff{i}qi{j}", [128, 512], F32)) for j in range(2)]
        ssum = [reg_ff.enter_context(
            nc.sbuf_tensor(f"ff{i}ss{j}", [128, 512], F32)) for j in range(2)]
        rt = [reg_ff.enter_context(
            nc.sbuf_tensor(f"ff{i}rt{j}", [128, 512], F32)) for j in range(2)]
        NU = 32

        with nc.Block(tag) as blk:
            @blk.tensor
            def _(te):
                for s, n in extra_te_waits:
                    te.wait_ge(s, n)
                te.wait_ge(prep_done[0], prep_done[1] - 8)
                for u in range(NU):
                    m, o = u // 2, u % 2
                    te.wait_ge(ln_done[0], ln_base + m + 1)
                    if u >= 2:
                        te.wait_ge(s_fsq, 2 * bU + 2 * u - 2)
                    for it in range(8):
                        te.matmul(ps_r[u % 2][:, :], xnTv[:, it, m * 128:(m + 1) * 128],
                                  Cvv[:, it, o * 512:(o + 1) * 512],
                                  start=(it == 0), stop=(it == 7))
                    if u == 0:
                        te.wait_ge(*prep_done)
                    for it in range(8):
                        ins = te.matmul(ps_i[u % 2][:, :],
                                        xnTv[:, it, m * 128:(m + 1) * 128],
                                        Svv[:, it, o * 512:(o + 1) * 512],
                                        start=(it == 0), stop=(it == 7))
                    ins.then_inc(s_fm, 1)
                    if ln_fuse and u % 2 == 0 and u >= 6:
                        L_transpose(te, u // 2 - 3)
                if ln_fuse:
                    for t in range(TT - 3, TT):
                        L_transpose(te, t)

            @blk.scalar
            def _(a):
                for u in range(NU):
                    a.wait_ge(s_fm, bU + u + 1)
                    if u >= 2:
                        a.wait_ge(s_fad, bU + u - 1)
                    a.activation(sq_r[u % 2][:, :], ps_r[u % 2][:, :],
                                 AF.Square).then_inc(s_fsq, 1)
                    a.activation(sq_i[u % 2][:, :], ps_i[u % 2][:, :],
                                 AF.Square).then_inc(s_fsq, 1)
                    if u >= 1:
                        a.wait_ge(s_fad, bU + u)
                        if u >= 2:
                            a.wait_ge(s_fxv, bU + u - 1)
                        a.activation(rt[(u - 1) % 2][:, :], ssum[(u - 1) % 2][:, :],
                                     AF.Sqrt).then_inc(s_fsr, 1)
                    if ln_fuse and u % 2 == 0 and u >= 2:
                        L_scalar(a, u // 2 - 1)
                        if u >= 6:
                            L_copy(a, u // 2 - 3)
                        if u >= 12 and (u - 12) % 8 == 0:
                            L_stage(a, (u - 12) // 8)
                a.wait_ge(s_fad, bU + NU)
                a.wait_ge(s_fxv, bU + NU - 1)
                a.activation(rt[(NU - 1) % 2][:, :], ssum[(NU - 1) % 2][:, :],
                             AF.Sqrt).then_inc(s_fsr, 1)
                if ln_fuse:
                    L_scalar(a, TT - 1)
                    for t in range(TT - 3, TT):
                        L_copy(a, t)
                    L_stage(a, 3)

            @blk.vector
            def _(v):
                for u in range(NU):
                    m, o = u // 2, u % 2
                    v.wait_ge(s_fsq, 2 * bU + 2 * u + 2)
                    if u >= 2:
                        v.wait_ge(s_fsr, bU + u - 1)
                    v.tensor_add(ssum[u % 2][:, :], sq_r[u % 2][:, :],
                                 sq_i[u % 2][:, :]).then_inc(s_fad, 1)
                    if u >= 1:
                        v.wait_ge(s_fsr, bU + u)
                        m1, o1 = (u - 1) // 2, (u - 1) % 2
                        v.tensor_add(xv[:, m1, o1 * 512:(o1 + 1) * 512],
                                     xv[:, m1, o1 * 512:(o1 + 1) * 512],
                                     rt[(u - 1) % 2][:, :]).then_inc(s_fxv, 1)
                    if ln_fuse and u % 2 == 0 and u >= 2:
                        L_stats(v, u // 2 - 1)
                        if u >= 4:
                            L_norm(v, u // 2 - 2)
                v.wait_ge(s_fsr, bU + NU)
                v.tensor_add(xv[:, 15, 512:1024], xv[:, 15, 512:1024],
                             rt[(NU - 1) % 2][:, :]).then_inc(s_fxv, 1)
                if ln_fuse:
                    L_stats(v, TT - 1)
                    L_norm(v, TT - 2)
                    L_norm(v, TT - 1)
        reg_ff.close()
        if ln_fuse:
            ffn_ln3[0] = (sL_cp, bTL + TT)
            reg_lnF.close()
        return (s_fxv, bU + NU)

    ffn_ln3 = [None]

    if PH >= 3:
        FS = _F_STAGE
        prep1 = emit_ffn_prep("fprep1", final_wait)
        afx = final_wait[0]
        if FS >= 1:
            ln2_done = emit_ln("ln2", [],
                               tile_waits=lambda t: [(afx, 2 * t + 2)],
                               keep_reg=True)
            final_wait = ln2_done[:2]
        if FS >= 2:
            ffn2_done = emit_ffn("ffn2", ln2_done[:2], prep1,
                                 ln_fuse=(PH >= 4))
            final_wait = ffn2_done
            ln2_done[2].close()
        prep1[3].close()

    nw = {}

    def alloc_bwq():
        reg_bW = ExitStack()
        nw["reg"] = reg_bW

        def sbW(name, shape, dtype):
            return reg_bW.enter_context(nc.sbuf_tensor(name, shape, dtype))
        nw["wqv"] = sbW("wqv", [128, 8 * 768], BF16)
        nw["wqvv"] = nw["wqv"][:, :].rearrange("p (i c) -> p i c", i=8)
        nw["bq6"] = sbW("bq6", [128, 6], F32)
        nw["bqr_s"] = sbW("bqr_s", [1, 256], F32)
        nw["bpr_s"] = sbW("bpr_s", [1, 1024], F32)
        nw["gixs"] = sbW("gixs", [128, 8], mybir.dt.int32)
        nw["onesbf"] = sbW("onesbf", [1, 128], BF16)
        nw["bprbf"] = sbW("bprbf", [1, 1024], BF16)
        nw["bqvbf"] = sbW("bqvbf", [1, 256], BF16)
        nw["wpv"] = sbW("wpv", [128, 8 * 1024], BF16)
        nw["wpvv"] = nw["wpv"][:, :].rearrange("p (i f) -> p i f", i=8)
        nw["xg"] = [sbW(f"bxg{i}", [128, 8 * 512], BF16) for i in range(2)]

    def emit_bwq(gate):
        stg = nw["stg"]
        NS = len(stg)
        s_bwl = [sem(f"b_wl{k}") for k in range(NS)]
        s_bwb = sem("b_wb")
        s_bwc = sem("b_wc")
        wqvv, wpvv = nw["wqvv"], nw["wpvv"]
        tiles_q = [(wqkv_d, 128 * (i // 3), 256 * (i % 3),
                    wqvv[:, i // 3, 256 * (i % 3):256 * (i % 3) + 256])
                   for i in range(24)]
        tiles_p = [(wproj_d, 128 * (i // 4), 256 * (i % 4),
                    wpvv[:, i // 4, 256 * (i % 4):256 * (i % 4) + 256])
                   for i in range(32)]
        def _bwq_loader(eng, q):
            eng.wait_ge(*gate)
            for i, (srcd, r0, c0, dstv) in enumerate(tiles_q + tiles_p):
                if i % 2 != q:
                    continue
                s = i % NS
                if i >= NS:
                    eng.wait_ge(s_bwc, i - NS + 4)
                eng.dma_start(out=stg[s],
                              in_=srcd[r0:r0 + 128, c0:c0 + 256]
                              ).then_inc(s_bwl[s], 16)

        with nc.Block("bwq") as blk:
            @blk.scalar
            def _(adm):
                adm.wait_ge(*gate)
                adm.dma_start(out=nw["bq6"][:, :], in_=bqkv_d[:, :]
                              ).then_inc(s_bwb, 16)
                adm.dma_start(out=nw["bqr_s"][:, :], in_=bqvr_d[:, :]
                              ).then_inc(s_bwb, 16)
                adm.dma_start(out=nw["bpr_s"][:, :], in_=bpr_d[:, :]
                              ).then_inc(s_bwb, 16)
                adm.dma_start(out=nw["gixs"][:, :], in_=gix_d[:, :]
                              ).then_inc(s_bwb, 16)
                _bwq_loader(adm, 0)

            @blk.sync
            def _(sy):
                _bwq_loader(sy, 1)

            @blk.vector
            def _(g_):
                g_.wait_ge(s_bwb, 64)
                g_.tensor_copy(nw["bprbf"][:, :], nw["bpr_s"][:, :]
                               ).then_inc(s_bwc, 1)
                g_.tensor_copy(nw["bqvbf"][:, :], nw["bqr_s"][:, :]
                               ).then_inc(s_bwc, 1)
                g_.memset(nw["onesbf"][:, :], 1.0).then_inc(s_bwc, 1)
                nl2 = [0] * NS
                allt = tiles_q + tiles_p
                for i, (srcd, r0, c0, dstv) in enumerate(allt):
                    s = i % NS
                    nl2[s] += 1
                    g_.wait_ge(s_bwl[s], 16 * nl2[s])
                    g_.tensor_copy(dstv, stg[s]).then_inc(s_bwc, 1)

    def emit_attn_b(ln3_done, gate_w):
        alloc_bwq()
        BS = _B_STAGE
        s_cp3 = ln3_done[0]
        cp3_base = ln3_done[1] - TT
        kTv = xnT[:, :].rearrange("p (j t) -> p j t", t=8192)

        reg_bP = ExitStack()

        def sbP(name, shape, dtype, reg):
            return reg.enter_context(nc.sbuf_tensor(name, shape, dtype))
        bq6 = nw["bq6"]
        gixs = nw["gixs"]
        onesbf = nw["onesbf"]
        wpvv = nw["wpvv"]
        wqvv = nw["wqvv"]
        bprbf = nw["bprbf"]
        bqvbf = nw["bqvbf"]
        negMp = sbP("negMp", [128, 2], F32, reg_bP)

        reg_bB = ExitStack()
        vtok = sbP("vtok", [128, 64 * 260], BF16, reg_bB)
        vtokv = vtok[:, :].rearrange("p (k f) -> p k f", f=260)
        qbuf = [sbP(f"qbuf{i}", [128, 2 * 512], BF16, reg_bB) for i in range(2)]
        qbv = [q[:, :].rearrange("p (j t) -> p j t", t=512) for q in qbuf]
        NE = 3
        E2 = [sbP(f"E2{i}", [128, 1024], BF16, reg_bB) for i in range(NE)]
        ctxb = sbP("ctxb", [128, 4 * 256], BF16, reg_bB)
        ctxbv = ctxb[:, :].rearrange("p (q f) -> p q f", f=256)
        ctxT = sbP("ctxT", [128, 2 * 512], BF16, reg_bB)
        ctxTv = ctxT[:, :].rearrange("p (fh t) -> p fh t", t=512)
        dtmp = sbP("dtmp", [128, 8], F32, reg_bB)

        reg_bA = ExitStack()
        xg = nw["xg"]
        xgv = [x_[:, :].rearrange("p (i t) -> p i t", i=8) for x_ in xg]
        qstage = sbP("bqs", [128, 2 * 512], BF16, reg_bA)
        qstgv = qstage[:, :].rearrange("p (j t) -> p j t", t=512)
        reg_bN = ExitStack()
        ps_m = reg_bN.enter_context(nc.psum_tensor("ps_m", [128, 256], F32))
        qmxa = sbP("qmxa", [128, 32], F32, reg_bN)
        kmxa = sbP("kmxa", [128, 32], F32, reg_bN)
        qmx2 = sbP("qmx2", [128, 2], F32, reg_bN)
        kmx2 = sbP("kmx2", [128, 2], F32, reg_bN)
        qsq2 = sbP("qsq2", [128, 2], F32, reg_bN)
        ksq2 = sbP("ksq2", [128, 2], F32, reg_bN)
        nrow = sbP("nrow", [1, 8], F32, reg_bN)

        reg_pj = ExitStack()
        ps_qk = [reg_pj.enter_context(nc.psum_tensor(f"ps_bqk{i}", [128, 512], F32))
                 for i in range(2)]
        ps_v = [reg_pj.enter_context(nc.psum_tensor(f"ps_bv{i}", [128, 256], F32))
                for i in range(2)]

        s_agi = [sem(f"b_agi{c}") for c in range(4)]
        s_agc = [sem(f"b_agc{c}") for c in range(4)]
        s_bwc = sem("b_wc")
        s_xgl = [sem(f"b_xgl{i}") for i in range(2)]
        s_pj = sem("b_pj")
        s_ev = sem("b_ev")
        s_qso = sem("b_qso")
        s_bnm = sem("b_nm")
        s_bnc = sem("b_nc")
        s_qbl = [sem(f"b_qbl{i}") for i in range(2)]
        s_bs2 = sem("b_s2")
        s_bex = sem("b_ex")
        s_bmk = sem("b_mk")
        s_bpv = sem("b_bpv")
        s_dnr = sem("b_dnr")
        s_bcx = sem("b_cx")
        s_ttc = sem("b_ttc")
        s_a2o = sem("b_a2o")
        s_a2c = [sem(f"b_a2c{d}") for d in range(4)]
        s_cgl = sem("b_cgl")
        s_bom = sem("b_om")
        s_box = sem("b_ox")

        units = []
        mask_thru = []
        pair_last = {}
        uof = {}
        mc = 0
        for g in range(16):
            for P in range(2):
                for j in range(4 * g + 3, -1, -1):
                    uu = len(units)
                    uof[(g, P, j)] = uu
                    if j == 0:
                        pair_last[(g, P)] = uu
                    if j >= 4 * g:
                        mc += 2
                    units.append((g, P, j))
                    mask_thru.append(mc)
        NU_B = len(units)

        with nc.Block("bag1") if BS >= 1 else _NullBlock() as blk:
            @blk.gpsimd
            def _(g_):
                for p in range(2):
                    g_.wait_ge(s_agi[2 * p], 16)
                    g_.wait_ge(s_agi[2 * p + 1], 16)
                    g_.collective_compute(
                        "AllGather", ALU.bypass, replica_groups=GROUPS,
                        ins=[ag1_in[2048 * p:2048 * (p + 1), :]],
                        outs=[ag1_out[8192 * p:8192 * (p + 1), :]],
                    ).then_inc(s_agc[p], 1)

        nw["stg"] = [e[:, :].bitcast(F32)[:, 256 * h:256 * (h + 1)]
                     for e in E2 for h in range(2)]
        emit_bwq(gate_w)

        def _bp_tid(u):
            return 4 * (u % 4) + u // 4

        with nc.Block("bproj") if BS >= 2 else _NullBlock() as blk:
            @blk.sync
            def _(sync):
                for u in range(16):
                    r, c = u % 4, u // 4
                    sync.wait_ge(s_agc[c // 2], 1)
                    if u >= 2:
                        sync.wait_ge(s_xgl[u % 2], 16 * (u // 2))
                        sync.wait_ge(s_pj, 8 * (u - 1))
                    row0 = 8192 * (c // 2) + 2048 * r + 1024 * (c % 2)
                    sync.dma_start(
                        out=xgv[u % 2],
                        in_=ag1_out[row0:row0 + 1024,
                                    :].rearrange("(ft p) tk -> p ft tk", p=128)
                    ).then_inc(s_xgl[u % 2], 16)

            @blk.tensor
            def _(te):
                te.wait_ge(s_cp3, cp3_base + 16)
                te.wait_ge(s_bwc, 27)
                for u in range(16):
                    te.wait_ge(s_xgl[u % 2], 16 * (u // 2 + 1))
                    for j in range(4):
                        if u > 0 or j >= 2:
                            guard = (8 * u + j - 1) if j >= 2 else (8 * u + j - 5)
                            te.wait_ge(s_ev, guard)
                        for it in range(8):
                            ins = te.matmul(ps_qk[j % 2][:, :],
                                            wqvv[:, it, j * 128:(j + 1) * 128],
                                            xgv[u % 2][:, it, :],
                                            start=(it == 0), stop=(it == 7))
                        ins.then_inc(s_pj, 1)
                    for i in range(4):
                        if u > 0 or i >= 2:
                            guard = (8 * u + i + 3) if i >= 2 else (8 * u + i - 1)
                            te.wait_ge(s_ev, guard)
                        for it in range(8):
                            te.matmul(ps_v[i % 2][:, :],
                                      xgv[u % 2][:, it, i * 128:(i + 1) * 128],
                                      wqvv[:, it, 512:768],
                                      start=(it == 0), stop=False)
                        te.matmul(ps_v[i % 2][:, :], onesbf[0:1, 0:128],
                                  bqvbf[0:1, :], start=False, stop=True
                                  ).then_inc(s_pj, 1)


            @blk.vector
            def _(v):
                for h in range(4):
                    v.memset(vtokv[:, :, h * 65 + 64:h * 65 + 65], 1.0)
                for u in range(16):
                    t = _bp_tid(u)
                    for j in range(2):
                        v.wait_ge(s_pj, 8 * u + j + 1)
                        if u >= 1:
                            v.wait_ge(s_qso, 32 * u)
                        v.tensor_scalar(qstgv[:, j, :], ps_qk[j % 2][:, :],
                                        bq6[:, j:j + 1], 0.125, ALU.add, ALU.mult
                                        ).then_inc(s_ev, 1)
                    for j in range(2, 4):
                        v.wait_ge(s_pj, 8 * u + j + 1)
                        if u == 0 and j == 2:
                            for c in range(4):
                                v.wait_ge(s_agi[c], 16)
                        v.tensor_scalar_add(kTv[:, j - 2, 512 * t:512 * (t + 1)],
                                            ps_qk[j % 2][:, :], bq6[:, j:j + 1]
                                            ).then_inc(s_ev, 1)
                    for i in range(4):
                        v.wait_ge(s_pj, 8 * u + 4 + i + 1)
                        for h in range(4):
                            ins = v.tensor_copy(
                                vtokv[:, 4 * t + i, h * 65:h * 65 + 64],
                                ps_v[i % 2][:, h * 64:(h + 1) * 64])
                        ins.then_inc(s_ev, 1)
                    v.wait_ge(s_ev, 8 * u + 4)
                    for j in range(2):
                        v.reduce_max(qmxa[:, j * 16 + t:j * 16 + t + 1],
                                     qstgv[:, j, :], axis=AXX,
                                     apply_absolute_value=True)
                    for j in range(2):
                        ins = v.reduce_max(kmxa[:, j * 16 + t:j * 16 + t + 1],
                                           kTv[:, j, 512 * t:512 * (t + 1)],
                                           axis=AXX, apply_absolute_value=True)
                    ins.then_inc(s_bnm, 1)

            @blk.gpsimd
            def _(g_):
                for u in range(16):
                    t = _bp_tid(u)
                    g_.wait_ge(s_ev, 8 * u + 2)
                    for jj in range(2):
                        g_.dma_start(
                            out=q_scr[128 * jj:128 * (jj + 1),
                                      512 * t:512 * (t + 1)],
                            in_=qstgv[:, jj, :]).then_inc(s_qso, 16)

        NOPS = [0]
        with nc.Block("bnrm") if BS >= 3 else _NullBlock() as blk:
            OPS = []

            def op(eng, fn):
                OPS.append((eng, fn))
            op("v", lambda v: v.reduce_max(qmx2[:, 0:1], qmxa[:, 0:16], axis=AXX))
            op("v", lambda v: v.reduce_max(qmx2[:, 1:2], qmxa[:, 16:32], axis=AXX))
            op("v", lambda v: v.reduce_max(kmx2[:, 0:1], kmxa[:, 0:16], axis=AXX))
            op("v", lambda v: v.reduce_max(kmx2[:, 1:2], kmxa[:, 16:32], axis=AXX))
            op("a", lambda a: a.activation(qsq2[:, :], qmx2[:, :], AF.Square))
            op("a", lambda a: a.activation(ksq2[:, :], kmx2[:, :], AF.Square))
            for j in range(2):
                for h01 in range(2):
                    op("t", lambda t, j=j, h01=h01: t.matmul(
                        ps_m[0:1, 2 * j + h01:2 * j + h01 + 1],
                        qsq2[h01 * 64:h01 * 64 + 64, j:j + 1],
                        onesc[h01 * 64:h01 * 64 + 64, 0:1],
                        start=True, stop=True))
                    op("t", lambda t, j=j, h01=h01: t.matmul(
                        ps_m[0:1, 4 + 2 * j + h01:4 + 2 * j + h01 + 1],
                        ksq2[h01 * 64:h01 * 64 + 64, j:j + 1],
                        onesc[h01 * 64:h01 * 64 + 64, 0:1],
                        start=True, stop=True))
            op("v", lambda v: v.tensor_copy(nrow[:, :], ps_m[0:1, 0:8]))
            op("v", lambda v: v.tensor_mul(nrow[0:1, 0:4], nrow[0:1, 0:4],
                                           nrow[0:1, 4:8]))
            op("a", lambda a: a.activation(nrow[0:1, 0:4], nrow[0:1, 0:4],
                                           AF.Sqrt))
            op("v", lambda v: v.tensor_scalar_mul(nrow[0:1, 0:4],
                                                  nrow[0:1, 0:4], -1.0))
            op("v", lambda v: v.tensor_tensor(nrow[0:1, 4:5], nrow[0:1, 0:1],
                                              nrow[0:1, 1:2], ALU.min))
            op("v", lambda v: v.tensor_tensor(nrow[0:1, 5:6], nrow[0:1, 2:3],
                                              nrow[0:1, 3:4], ALU.min))
            for P in range(2):
                op("t", lambda t, P=P: t.matmul(
                    ps_m[:, 8 + P:9 + P], ones1f[0:1, :], nrow[0:1, 4 + P:5 + P],
                    start=True, stop=True))
            for P in range(2):
                op("v", lambda v, P=P: v.tensor_copy(negMp[:, P:P + 1],
                                                     ps_m[:, 8 + P:9 + P]))
            NOPS[0] = len(OPS)

            def run_ops(eng, code):
                eng.wait_ge(s_bnm, 16)
                for k, (e, fn) in enumerate(OPS):
                    if e == code:
                        eng.wait_ge(s_bnc, k)
                        fn(eng).then_inc(s_bnc, 1)

            @blk.vector
            def _(v):
                run_ops(v, "v")

            @blk.scalar
            def _(a):
                run_ops(a, "a")

            @blk.tensor
            def _(te):
                run_ops(te, "t")
        reg_pj.close()
        reg_bN.close()
        reg_bA.close()

        reg_at = ExitStack()
        ps2 = [reg_at.enter_context(nc.psum_tensor(f"ps_b2{i}", [128, 1024], F32))
               for i in range(3)]
        ctxps = [reg_at.enter_context(nc.psum_tensor(f"ps_bc{i}", [128, 512], F32))
                 for i in range(2)]
        ps_tr = ps2[0][:, 0:128].bitcast(BF16)

        def pv_of(te, uu):
            g0, P0, j0 = units[uu]
            te.wait_ge(s_bex, uu + 1)
            if j0 >= 4 * g0:
                te.wait_ge(s_bmk, mask_thru[uu])
            pi = 2 * g0 + P0
            if j0 == 4 * g0 + 3 and pi >= 1:
                te.wait_ge(s_bcx, 8 * pi)
            last = None
            for qt in range(4):
                Q = 4 * g0 + qt
                if j0 > Q:
                    continue
                base = (qt % 2) * 130
                for h01 in range(2):
                    last = te.matmul(
                        ctxps[qt // 2][:, base + h01 * 65:base + h01 * 65 + 65],
                        E2[uu % NE][:, h01 * 512 + qt * 128:
                                   h01 * 512 + (qt + 1) * 128],
                        vtokv[:, j0, (2 * P0 + h01) * 65:(2 * P0 + h01) * 65 + 65],
                        start=(j0 == Q and qt % 2 == 1 and h01 == 0),
                        stop=(j0 == 0))
            last.then_inc(s_bpv, 1)

        with nc.Block("battn") if BS >= 4 else _NullBlock() as blk:
            @blk.gpsimd
            def _(g_):
                for g in range(16):
                    if g >= 2:
                        g_.wait_ge(s_qbl[g % 2], 32 * (g // 2))
                        g_.wait_ge(s_ttc, 8 * (g - 1))
                    g_.wait_ge(s_qso, 32 * (4 * (g % 4) + g // 4 + 1))
                    for jj in range(2):
                        g_.dma_start(
                            out=qbv[g % 2][:, jj, :],
                            in_=q_scr[128 * jj:128 * (jj + 1),
                                      512 * g:512 * (g + 1)]
                        ).then_inc(s_qbl[g % 2], 16)
                    d, h4 = g // 4, g % 4
                    g_.wait_ge(s_a2o, 32 * g)
                    g_.wait_ge(s_ttc, 8 * (g + 1))
                    for fh in range(2):
                        g_.dma_start(
                            out=ag2_in[256 * d + 128 * fh:
                                       256 * d + 128 * (fh + 1),
                                       512 * h4:512 * (h4 + 1)],
                            in_=ctxTv[:, fh, :]).then_inc(s_a2o, 16)
                    if h4 == 3:
                        g_.wait_ge(s_a2o, 128 * (d + 1))
                        g_.collective_compute(
                            "AllGather", ALU.bypass, replica_groups=GROUPS,
                            ins=[ag2_in[256 * d:256 * (d + 1), :]],
                            outs=[ag2_out[1024 * d:1024 * (d + 1), :]],
                        ).then_inc(s_a2c[d], 1)

            @blk.tensor
            def _(te):
                te.wait_ge(s_bnc, NOPS[0])
                uu = 0
                qtg = 0
                for g in range(16):
                    te.wait_ge(s_qbl[g % 2], 32 * (g // 2 + 1))
                    te.wait_ge(s_ttc, 8 * g)
                    for P in range(2):
                        pend = None
                        for j in range(4 * g + 3, -1, -1):
                            if uu >= 3:
                                te.wait_ge(s_bex, uu - 2)
                            te.matmul(ps2[uu % 3][:, 0:512],
                                      kTv[0:64, P, j * 128:(j + 1) * 128],
                                      qbv[g % 2][0:64, P, :],
                                      start=True, stop=True,
                                      tile_position=(0, 0))
                            te.matmul(ps2[uu % 3][:, 512:1024],
                                      kTv[64:128, P, j * 128:(j + 1) * 128],
                                      qbv[g % 2][64:128, P, :],
                                      start=True, stop=True,
                                      tile_position=(64, 0)).then_inc(s_bs2, 1)
                            if pend is not None:
                                pv_of(te, pend)
                            pend = uu
                            uu += 1
                        pv_of(te, pend)
                    te.wait_ge(s_bcx, 16 * g + 16)
                    for qt in range(4):
                        te.wait_ge(s_ttc, 2 * qtg)
                        te.transpose(ps_tr[:, 0:128],
                                     ctxbv[:, qt, 0:128], ident[:, :])
                        te.transpose(ps_tr[:, 128:256],
                                     ctxbv[:, qt, 128:256], ident[:, :]
                                     ).then_inc(s_ttc, 1)
                        qtg += 1

            @blk.scalar
            def _(a):
                a.wait_ge(s_bnc, NOPS[0])
                for uu, (g, P, j) in enumerate(units):
                    a.wait_ge(s_bs2, uu + 1)
                    if uu >= NE:
                        a.wait_ge(s_bpv, uu - NE + 1)
                    a.activation(E2[uu % NE][:, :], ps2[uu % 3][:, :], AF.Exp,
                                 bias=negMp[:, P:P + 1]).then_inc(s_bex, 1)

            @blk.vector
            def _(v):
                dc = [0]
                for g in range(16):
                    for P in range(2):
                        for j in range(4 * g + 3, 4 * g - 1, -1):
                            uu = uof[(g, P, j)]
                            i = j - 4 * g
                            v.wait_ge(s_bex, uu + 1)
                            v.tensor_mul(E2[uu % NE][:, i * 128:(i + 1) * 128],
                                         E2[uu % NE][:, i * 128:(i + 1) * 128],
                                         tri01[:, :]).then_inc(s_bmk, 1)
                            v.tensor_mul(E2[uu % NE][:, 512 + i * 128:
                                                    512 + (i + 1) * 128],
                                         E2[uu % NE][:, 512 + i * 128:
                                                    512 + (i + 1) * 128],
                                         tri01[:, :]).then_inc(s_bmk, 1)
                        v.wait_ge(s_bpv, pair_last[(g, P)] + 1)
                        for half in range(2):
                            ins = v.tensor_single_scalar(
                                dtmp[:, 4 * half:4 * half + 4],
                                ctxps[half][:, 64:324].rearrange(
                                    "p (k s) -> p k s", s=65)[:, :, 0:1
                                ].squeeze(2),
                                1e-30, ALU.add)
                        dc[0] += 1
                        ins.then_inc(s_dnr, 1)
                        v.wait_ge(s_dnr, 2 * dc[0] - 1)
                        v.reciprocal(dtmp[:, :], dtmp[:, :]).then_inc(s_dnr, 1)
                        v.wait_ge(s_dnr, 2 * dc[0])
                        for qt in range(4):
                            base = (qt % 2) * 130
                            for h01 in range(2):
                                v.tensor_scalar_mul(
                                    ctxbv[:, qt,
                                          (2 * P + h01) * 64:(2 * P + h01 + 1) * 64],
                                    ctxps[qt // 2][:, base + h01 * 65:
                                                   base + h01 * 65 + 64],
                                    dtmp[:, 2 * qt + h01:2 * qt + h01 + 1]
                                ).then_inc(s_bcx, 1)
                    for qt in range(4):
                        Qg = 4 * g + qt
                        qb, l = Qg // 4, Qg % 4
                        v.wait_ge(s_ttc, 2 * Qg + 1)
                        if qb >= 1 and l == 0:
                            v.wait_ge(s_a2o, 32 * qb)
                        v.tensor_copy(
                            ctxTv[:, :, l * 128:(l + 1) * 128],
                            ps_tr[:, :].rearrange("p (fh c) -> p fh c", c=128)
                        ).then_inc(s_ttc, 1)

        reg_at.close()
        reg_bB.close()
        reg_bC = ExitStack()
        ctxg = sbP("ctxg", [128, 8 * 2048], BF16, reg_bC)
        flshb = sqdump[:, 0:512]
        ctxgv = ctxg[:, :].rearrange("p (i t) -> p i t", i=8)
        ps_op = [reg_bC.enter_context(nc.psum_tensor(f"ps_bo{i}", [128, 512], F32))
                 for i in range(2)]

        with nc.Block("bout") if BS >= 5 else _NullBlock() as blk:
            @blk.gpsimd
            def _(g_):
                g_.wait_ge(sem("b_wb"), 64)
                for d in range(4):
                    g_.wait_ge(s_a2c[d], 1)
                for ft in range(8):
                    if ft >= 1:
                        g_.wait_ge(s_cgl, 16 * ft)
                    g_.indirect_dma_start(
                        out=ctxgv[:, ft, :], out_offset=None,
                        in_=ag2_out[0:128, :],
                        in_offset=bass.IndirectOffsetOnAxis(gixs[:, ft:ft + 1], 0),
                    ).then_inc(s_cgl, 16)
                g_.wait_ge(s_cgl, 128)
                g_.dma_start(out=flshb[:, :], in_=ag2_out[0:128, 0:512]
                             ).then_inc(s_cgl, 16)

            @blk.tensor
            def _(te):
                te.wait_ge(s_cgl, 144)
                te.wait_ge(s_bwc, 59)
                te.wait_ge(s_bex, NU_B)
                te.wait_ge(s_bcx, 256)
                te.wait_ge(s_ttc, 128)
                for seq in range(32):
                    u2 = seq
                    m, o = u2 // 2, u2 % 2
                    if seq >= 2:
                        te.wait_ge(s_box, seq - 1)
                    for ft in range(8):
                        te.matmul(ps_op[seq % 2][:, :],
                                  ctxgv[:, ft, m * 128:(m + 1) * 128],
                                  wpvv[:, ft, o * 512:(o + 1) * 512],
                                  start=(ft == 0), stop=False)
                    te.matmul(ps_op[seq % 2][:, :], onesbf[0:1, 0:128],
                              bprbf[0:1, o * 512:(o + 1) * 512],
                              start=False, stop=True).then_inc(s_bom, 1)

            @blk.vector
            def _(v):
                for seq in range(32):
                    u2 = seq
                    m, o = u2 // 2, u2 % 2
                    v.wait_ge(s_bom, seq + 1)
                    v.tensor_add(xv[:, m, o * 512:(o + 1) * 512],
                                 xv[:, m, o * 512:(o + 1) * 512],
                                 ps_op[seq % 2][:, :]).then_inc(s_box, 1)
        dbg["vtok"] = (vtok[:, 0:16384], 16384)
        dbg["negm"] = (negMp[:, 0:2].bitcast(BF16), 4)
        dbg["kt"] = (xnT[:, 0:16384], 16384)
        dbg["qscr"] = (q_scr[:, :].rearrange("(jj p) t -> p jj t", p=128), -8192)
        dbg["ctxg"] = (ctxg[:, 0:16384], 16384)
        reg_bC.close()
        reg_bP.close()
        nw["reg"].close()
        if BS >= 5:
            return (s_box, 32)
        if BS == 4:
            return (s_ttc, 128)
        if BS == 3:
            return (s_bnc, NOPS[0])
        if BS == 2:
            return (s_ev, 128)
        return ln3_done

    if PH >= 4:
        fxv = final_wait[0]
        fxv_base = final_wait[1] - 32
        ln3_done = ffn_ln3[0]
        battn_done = emit_attn_b(ln3_done, gate_w=(fxv, fxv_base + 32))
        final_wait = battn_done

    if PH >= 5:
        prep2 = emit_ffn_prep("fprep2", (sems["b_om"], 32))
        box = final_wait[0]
        ln4_done = emit_ln("ln4", [],
                           tile_waits=lambda t: [(box, 2 * t + 2)],
                           keep_reg=True)
        ffn4_done = emit_ffn("ffn4", ln4_done[:2], prep2,
                             extra_te_waits=[(box, 32)])
        final_wait = ffn4_done
        ln4_done[2].close()
        prep2[3].close()

    if _DUMP is not None:
        dbg_d = nc.dram_tensor("dbg", [128, 16384], BF16, kind="ExternalOutput")
        dbg["xnt"] = (xnT[:, 0:16384], 16384)
        src, ncols = dbg[_DUMP]
        if ncols < 0:
            out_ap = dbg_d[:, 0:16384].rearrange("p (a b) -> p a b", b=-ncols)
        else:
            out_ap = dbg_d[:, 0:ncols]
        sdmp = sem("dmp")
        with nc.Block("dump") as blk:
            @blk.gpsimd
            def _(g):
                g.wait_ge(*final_wait)
                g.dma_start(out=out_ap, in_=src
                            ).then_inc(sdmp, 16)
                g.wait_ge(sdmp, 16)

    stv = sem("stv")
    with nc.Block("store") as blk:
        @blk.sync
        def _(sync):
            sync.wait_ge(*final_wait)
            sync.dma_start(out=out_d.rearrange("(t p) f -> p t f", p=128),
                           in_=xv).then_inc(stv, 16)
            sync.wait_ge(stv, 16)

    ctx.close()
    return nc
